# version 37
# baseline (speedup 1.0000x reference)
import numpy as np
import ml_dtypes
import concourse.bass as bass
import concourse.mybir as mybir
from concourse.bass_utils import run_bass_kernel_spmd

F32 = mybir.dt.float32
BF16 = mybir.dt.bfloat16
AF = mybir.ActivationFunctionType
ALU = mybir.AluOpType
AX = mybir.AxisListType

ENGINES = ("sync", "scalar", "vector", "tensor", "gpsimd")

E = 1024
H = 16
FF = 4096
DT = 8
FT = 32
SC = 256
QCW = 512
KCW = 128


class Prog:

    def __init__(self, nc):
        self.nc = nc
        self.ops = {e: [] for e in ENGINES}
        self.count = {}
        self.waited = {}
        self.sems = {}

    def sem(self, name):
        h = self.nc.ctx.enter_context(self.nc.semaphore(name))
        self.count[name] = 0
        self.sems[name] = h
        return h

    def now(self, sem):
        return self.count[sem.name]

    def wait(self, eng, sem, val=None):
        if val is None:
            val = self.count[sem.name]
        assert val <= self.count[sem.name], (sem.name, val, self.count[sem.name])
        if val <= 0:
            return
        key = (eng, sem.name)
        if self.waited.get(key, 0) >= val:
            return
        self.waited[key] = val
        self.ops[eng].append(("wait", sem, val))

    def emit(self, eng, fn, inc=None, inc_n=1):
        if inc is not None:
            self.count[inc.name] += inc_n
        self.ops[eng].append(("op", fn, inc, inc_n))
        return self.count[inc.name] if inc is not None else None

    def dma(self, eng, sem, dst, src, **kw):
        self.wait(eng, sem, self.count[sem.name])
        return self.emit(eng, lambda e: e.dma_start(dst, src, **kw), inc=sem, inc_n=16)

    def barrier(self):
        for eng in ENGINES:
            for name, cnt in self.count.items():
                if cnt > 0:
                    key = (eng, name)
                    if self.waited.get(key, 0) < cnt:
                        self.waited[key] = cnt
                        self.ops[eng].append(("wait", self.sems[name], cnt))

    def build(self):
        nc = self.nc
        with nc.Block() as block:
            def mk(eng_name):
                def run(engine):
                    for item in self.ops[eng_name]:
                        if item[0] == "wait":
                            engine.wait_ge(item[1], item[2])
                        else:
                            _, fn, inc, inc_n = item
                            inst = fn(engine)
                            if inc is not None:
                                inst.then_inc(inc, inc_n)
                return run
            block.sync(mk("sync"))
            block.scalar(mk("scalar"))
            block.vector(mk("vector"))
            block.tensor(mk("tensor"))
            block.gpsimd(mk("gpsimd"))


def build_nc(S=4096):
    assert S % 512 == 0
    NCH = S // SC
    NQ = S // QCW
    NSS = S // KCW

    nc = bass.Bass("TRN2", target_bir_lowering=False)

    xT_in = nc.dram_tensor("xT_in", [E, S], F32, kind="ExternalInput").ap()
    W1a = nc.dram_tensor("W1a", [E, 3 * E], F32, kind="ExternalInput").ap()
    b1a = nc.dram_tensor("b1a", [3 * E], F32, kind="ExternalInput").ap()
    Wp = nc.dram_tensor("Wp", [E, E], F32, kind="ExternalInput").ap()
    bp = nc.dram_tensor("bp", [E], F32, kind="ExternalInput").ap()
    W1f = nc.dram_tensor("W1f", [FF, E], F32, kind="ExternalInput").ap()
    b1f = nc.dram_tensor("b1f", [FF], F32, kind="ExternalInput").ap()
    W2f = nc.dram_tensor("W2f", [E, FF], F32, kind="ExternalInput").ap()
    b2f = nc.dram_tensor("b2f", [E], F32, kind="ExternalInput").ap()
    outT = nc.dram_tensor("outT", [E, S], F32, kind="ExternalOutput").ap()

    x1_hbm = nc.dram_tensor("x1_hbm", [E, S], F32, kind="Internal").ap()
    x2_hbm = nc.dram_tensor("x2_hbm", [E, S], F32, kind="Internal").ap()
    a_hbm = nc.dram_tensor("a_hbm", [E, S], BF16, kind="Internal").ap()
    w1t_hbm = nc.dram_tensor("w1t_hbm", [128, DT * FF], BF16, kind="Internal").ap()
    w2t_hbm = nc.dram_tensor("w2t_hbm", [128, FT * E], BF16, kind="Internal").ap()

    def xchunk_ap(dram, c0, width):
        return dram.rearrange("(t p) s -> p t s", p=128)[:, :, c0:c0 + width]

    base = (nc.sbuf_base + 31) & ~31
    cap = nc.sbuf_base + nc.sbuf_bytes_remaining - base
    A_OFF = base
    B_OFF = base + 128 * 1024
    P_OFF = base + ((cap - 2560) & ~31)

    def at(name, shape, dt, off):
        t = nc.alloc_sbuf_tensor_at(name, shape, dt, offset=off)
        return t

    po = [P_OFF]

    def pat(name, shape, dt, size):
        t = at(name, shape, dt, po[0])
        po[0] += (size + 31) & ~31
        return t

    cst = pat("cst", [128, 8], F32, 32)
    ones_bf = pat("ones_bf", [128, 64], BF16, 128)
    ones_f = pat("ones_f", [128, 128], F32, 512)
    ident_f = pat("ident_f", [128, 128], F32, 512)
    ident_bf = pat("ident_bf", [128, 128], BF16, 256)
    b1f_sb = pat("b1f_sb", [128, 32], F32, 128)
    hb2f_sb = pat("hb2f_sb", [128, 8], F32, 32)
    bqk_sb = pat("bqk_sb", [128, 16], F32, 64)
    bp_sb = pat("bp_sb", [128, 8], F32, 32)
    bstage = pat("bstage", [32, 128], F32, 512)
    thr_sb = pat("thr_sb", [128, 4], F32, 32)
    scal_sb = pat("scal_sb", [1, 8], F32, 32)
    assert po[0] - base <= cap, po[0] - base

    W1fT = at("W1fT", [128, DT * FF], BF16, A_OFF)
    W2fT = at("W2fT", [128, FT * E], BF16, A_OFF + 64 * 1024)

    ao = A_OFF
    y2T = at("y2T", [128, DT * S], BF16, ao); ao += DT * S * 2
    qk_sb = [at("qk_sb0", [128, S], BF16, ao), at("qk_sb1", [128, S], BF16, ao + S * 2)]
    ao += 2 * S * 2
    kq_sb = [at("kq_sb0", [128, S], BF16, ao), at("kq_sb1", [128, S], BF16, ao + S * 2)]
    ao += 2 * S * 2
    v4a = at("v4a", [128, NSS * 4 * 65], BF16, ao); ao += NSS * 4 * 65 * 2
    pq_sb = [at("pq0", [128, 1024], BF16, ao), at("pq1", [128, 1024], BF16, ao + 2048),
             at("pq2", [128, 1024], BF16, ao + 4096),
             at("pq3", [128, 1024], BF16, ao + 6144)]
    ao += 8192
    assert ao <= A_OFF + 128 * 1024, ao - A_OFF

    bo = B_OFF
    x0c = []
    for i in range(3):
        x0c.append(at(f"x0c{i}", [128, DT * SC], F32, bo)); bo += DT * SC * 4
    xbc = []
    for i in range(2):
        xbc.append(at(f"xbc{i}", [128, DT * SC], BF16, bo)); bo += DT * SC * 2
    xsqc = at("xsqc", [128, DT * SC], BF16, bo); bo += DT * SC * 2
    yc = []
    for i in range(2):
        yc.append(at(f"yc{i}", [128, DT * SC], BF16, bo)); bo += DT * SC * 2
    hc = at("hc", [128, FT * SC], BF16, bo); bo += FT * SC * 2
    x1oc = at("x1oc", [128, DT * SC], F32, bo); bo += DT * SC * 4
    ec = [at("ec0", [128, SC], F32, bo), at("ec1", [128, SC], F32, bo + SC * 4)]
    bo += 2 * SC * 4
    ntmp = [at(f"nt{i}", [128, SC], F32, bo + i * SC * 4) for i in range(3)]
    bo += 3 * SC * 4
    stat_m = at("stat_m", [1, SC], F32, bo); bo += SC * 4
    stat_v = at("stat_v", [1, SC], F32, bo); bo += SC * 4
    stat_sq = at("stat_sq", [1, SC], F32, bo); bo += SC * 4
    stat_m2 = at("stat_m2", [1, SC], F32, bo - SC * 4)
    stat_r = at("stat_r", [1, SC], F32, bo); bo += SC * 4
    assert bo <= P_OFF, bo - B_OFF

    po0 = B_OFF
    w1b = [at("w1b0", [128, FF], F32, po0), at("w1b1", [128, FF], F32, po0 + FF * 4)]
    po0 += 2 * FF * 4
    w2b = at("w2b", [128, FF], F32, po0); po0 += FF * 4
    w2c = at("w2c", [128, FF], F32, po0); po0 += FF * 4
    rnd = []
    for ri in range(6):
        a_ = at(f"rndA{ri}", [128, 512], BF16, po0); po0 += 512 * 2
        b_ = at(f"rndB{ri}", [128, 512], BF16, po0); po0 += 512 * 2
        rnd.append((a_, b_))
    parts = at("parts", [128, 36], F32, po0); po0 += 36 * 4
    assert po0 <= P_OFF

    po2 = B_OFF
    x1c2 = [at("x1c2_0", [128, DT * SC], F32, po2),
            at("x1c2_1", [128, DT * SC], F32, po2 + DT * SC * 4)]
    po2 += 2 * DT * SC * 4
    xbc2 = [at("xbc2_0", [128, DT * SC], BF16, po2),
            at("xbc2_1", [128, DT * SC], BF16, po2 + DT * SC * 2)]
    po2 += 2 * DT * SC * 2
    xsqc2 = at("xsqc2", [128, DT * SC], BF16, po2); po2 += DT * SC * 2

    pm = B_OFF + 28 * 1024
    mask_sb = at("mask_sb", [128, 2048], BF16, pm); pm += 4096
    bvrep_sb = at("bvrep_sb", [128, E], F32, pm); pm += E * 4
    p3o = pm
    ast_sb = [at("ast0", [64, QCW], BF16, p3o), at("ast1", [64, QCW], BF16, p3o + QCW * 2)]
    p3o += 2 * QCW * 2
    wqk_sb = [at("wqk0", [128, DT * 128], BF16, p3o),
              at("wqk1", [128, DT * 128], BF16, p3o + DT * 128 * 2)]
    p3o += 2 * DT * 128 * 2
    wv4_sb = at("wv4", [128, DT * 256], BF16, p3o); p3o += DT * 256 * 2
    v4b = at("v4b", [128, NSS * 4 * 65], BF16, p3o); p3o += NSS * 4 * 65 * 2
    ntmp2 = None
    assert p3o <= P_OFF
    pz = B_OFF
    zst = at("zst", [128, 512], F32, pz); pz += 512 * 4
    zc2 = at("zc2", [128, 32], F32, pz); pz += 32 * 4
    rz2 = at("rz2", [128, 32], BF16, pz); pz += 32 * 2 + 32
    astg = [at("astg0", [64, NQ * QCW], BF16, pz),
            at("astg1", [64, NQ * QCW], BF16, pz + NQ * QCW * 2)]
    pz += 2 * NQ * QCW * 2
    rz_sb = at("rz_sb", [NQ, 512], BF16, pz); pz += 512 * 2
    rzq = at("rzq", [1, NQ * 512], BF16, pz); pz += NQ * 512 * 2
    assert pz <= B_OFF + 28 * 1024, pz - B_OFF

    po4 = B_OFF
    a4c = [at("a4c0", [128, DT * SC], BF16, po4),
           at("a4c1", [128, DT * SC], BF16, po4 + DT * SC * 2)]
    po4 += 2 * DT * SC * 2
    x1c4 = [at("x1c4_0", [128, DT * SC], F32, po4),
            at("x1c4_1", [128, DT * SC], F32, po4 + DT * SC * 4)]
    po4 += 2 * DT * SC * 4
    x2oc = [at("x2oc0", [128, DT * SC], F32, po4),
            at("x2oc1", [128, DT * SC], F32, po4 + DT * SC * 4)]
    po4 += 2 * DT * SC * 4
    wp_sb = at("wp_sb", [128, DT * E], BF16, po4); po4 += DT * E * 2
    e4c = [at("e4c0", [128, SC], F32, po4), at("e4c1", [128, SC], F32, po4 + SC * 4)]
    po4 += 2 * SC * 4
    assert po4 <= P_OFF, po4 - B_OFF

    PPbig = nc.alloc_psum_tensor("PPbig", [128, 2048], F32)
    PPm = nc.alloc_psum_tensor("PPm", [128, 1024], F32)
    PPn = nc.alloc_psum_tensor("PPn", [128, 1024], F32)

    p = Prog(nc)
    sT = p.sem("sT"); sS = p.sem("sS"); sV = p.sem("sV"); sG = p.sem("sG")
    sLx = [p.sem("sLx0"), p.sem("sLx1"), p.sem("sLx2")]
    sLxb = [p.sem("sLxb0"), p.sem("sLxb1")]
    sLa = [p.sem("sLa0"), p.sem("sLa1")]
    sSt = [p.sem("sSt0"), p.sem("sSt1")]
    sSA = [p.sem("sSA0"), p.sem("sSA1")]
    sWst = p.sem("sWst")
    sTr = [p.sem("sTr0"), p.sem("sTr1")]
    sGw = [p.sem("sGw0"), p.sem("sGw1")]
    sZst = [p.sem("sZst0"), p.sem("sZst1")]
    sRzQ = [p.sem("sRzQ0"), p.sem("sRzQ1")]
    sW2L = [p.sem("sW2L0"), p.sem("sW2L1"), p.sem("sW2L2")]
    sWst2 = p.sem("sWst2")

    W = p.wait
    EM = p.emit

    for col, val in [(0, 1.0 / E), (1, -12.0), (2, 0.125), (3, 0.5),
                     (4, 1e-5), (5, 1.0 / (FF * E)), (6, 0.25)]:
        EM("gpsimd", lambda e, col=col, val=val: e.memset(cst[:, col:col + 1], val), inc=sG)
    EM("gpsimd", lambda e: e.memset(ones_bf[:], 1.0), inc=sG)
    EM("gpsimd", lambda e: e.memset(ones_f[:], 1.0), inc=sG)
    EM("gpsimd", lambda e: e.memset(ident_f[:], 0.0), inc=sG)
    EM("gpsimd", lambda e: e.memset(ident_bf[:], 0.0), inc=sG)
    W("gpsimd", sG)
    EM("gpsimd", lambda e: e.affine_select(
        out=ident_f[:], in_=ident_f[:], compare_op=ALU.not_equal,
        fill=1.0, base=0, pattern=[[-1, 128]], channel_multiplier=1), inc=sG)
    EM("gpsimd", lambda e: e.affine_select(
        out=ident_bf[:], in_=ident_bf[:], compare_op=ALU.not_equal,
        fill=1.0, base=0, pattern=[[-1, 128]], channel_multiplier=1), inc=sG)
    mPrepG = p.now(sG)

    def bias_transpose(src_ap, rows, dst, half):
        W("sync", sT)
        mL = p.dma("sync", sLx[0], bstage[0:rows, 0:128], src_ap)
        W("tensor", sLx[0], mL)
        W("tensor", sG, mPrepG)
        W("tensor", sS)
        EM("tensor", lambda e, rows=rows: e.transpose(
            PPm[0:128, 0:rows], bstage[0:rows, 0:128], ident_f[0:rows, 0:rows]), inc=sT)
        W("scalar", sT)
        if half:
            EM("scalar", lambda e, rows=rows, dst=dst: e.activation(
                dst[:, 0:rows], PPm[0:128, 0:rows], AF.Identity, scale=cst[:, 3:4]), inc=sS)
        else:
            EM("scalar", lambda e, rows=rows, dst=dst: e.activation(
                dst[:, 0:rows], PPm[0:128, 0:rows], AF.Identity), inc=sS)

    bias_transpose(b1f.rearrange("(t p) -> t p", p=128), 32, b1f_sb, False)
    bias_transpose(b2f.rearrange("(t p) -> t p", p=128), 8, hb2f_sb, True)
    bias_transpose(bp.rearrange("(t p) -> t p", p=128), 8, bp_sb, False)
    bias_transpose(b1a.rearrange("(h c) -> h c", c=192)[:, 0:128], 16, bqk_sb, False)

    def tern_threshold(pcol0, n_loads, thr_col, mP1, me="vector", ms=None,
                       ss_wait=None):
        if ms is None:
            ms = sV
        sb = 4 * thr_col
        tb = 2 * thr_col
        vece = me != "gpsimd"
        ev = "vector" if vece else "scalar"
        es = sV if vece else sS
        W("vector", sV, mP1)
        mTot = EM("vector", lambda e: e.reduce_sum(
            parts[:, 32 + thr_col:33 + thr_col],
            parts[:, pcol0:pcol0 + n_loads], AX.X), inc=sV)
        W("tensor", sV, mTot)
        W("tensor", sG, mPrepG)
        if ss_wait is not None:
            W("tensor", sS, ss_wait)
        mPs = EM("tensor", lambda e: e.matmul(
            PPn[0:1, thr_col:thr_col + 1], parts[:, 32 + thr_col:33 + thr_col],
            ones_f[:, 0:1], start=True, stop=True), inc=sT)
        W(ev, sT, mPs)
        if vece:
            mSc = EM(ev, lambda e: e.tensor_scalar(
                scal_sb[0:1, sb:sb + 1], PPn[0:1, thr_col:thr_col + 1],
                1.0 / (FF * E), None, ALU.mult), inc=es)
        else:
            mSc = EM(ev, lambda e: e.activation(
                scal_sb[0:1, sb:sb + 1], PPn[0:1, thr_col:thr_col + 1],
                AF.Identity, scale=cst[0:1, 5:6]), inc=es)
        W(me, es, mSc)
        EM(me, lambda e: e.tensor_scalar(
            scal_sb[0:1, sb + 1:sb + 2], scal_sb[0:1, sb:sb + 1], 1e-8, None,
            ALU.max), inc=ms)
        W(me, ms)
        EM(me, lambda e: e.tensor_scalar(
            scal_sb[0:1, sb + 2:sb + 3], scal_sb[0:1, sb + 1:sb + 2], 0.5, None,
            ALU.mult), inc=ms)
        W(me, ms)
        EM(me, lambda e: e.tensor_scalar(
            scal_sb[0:1, sb + 3:sb + 4], scal_sb[0:1, sb + 1:sb + 2], -0.5, None,
            ALU.mult), inc=ms)
        mThr = p.now(ms)
        W("tensor", ms, mThr)
        mBc = EM("tensor", lambda e: e.matmul(
            PPn[0:128, 512 + tb:514 + tb], ones_f[0:1, :],
            scal_sb[0:1, sb + 2:sb + 4], start=True, stop=True), inc=sT)
        W(ev, sT, mBc)
        if vece:
            mThrS = EM(ev, lambda e: e.tensor_copy(
                thr_sb[:, tb:tb + 2], PPn[0:128, 512 + tb:514 + tb]), inc=es)
        else:
            mThrS = EM(ev, lambda e: e.activation(
                thr_sb[:, tb:tb + 2], PPn[0:128, 512 + tb:514 + tb],
                AF.Identity), inc=es)
        return es, mThrS

    def w1f_src(ld):
        return W1f.rearrange("(b p) d -> p b d", p=128)[:, ld * 4:(ld + 1) * 4, :]

    def w1b_dst(buf):
        return buf[:].rearrange("p (a d) -> p a d", a=4)

    mRed1, mRed2, mLW2, mLd1 = {}, {}, {}, {}
    for ld in range(8):
        if ld >= 2:
            W("sync", sV, mRed1[ld - 2])
        mL = p.dma("sync", sLx[ld % 2], w1b_dst(w1b[ld % 2]), w1f_src(ld))
        mLd1[ld] = mL
        W("vector", sLx[ld % 2], mL)
        mRed1[ld] = EM("vector", lambda e, ld=ld: e.reduce_sum(
            parts[:, ld:ld + 1], w1b[ld % 2][:], AX.X,
            apply_absolute_value=True), inc=sV)
        if ld >= 2:
            W("scalar", sV, mRed2[ld - 2])
        w2buf = w2b if ld % 2 == 0 else w2c
        mLW2[ld] = p.dma("scalar", sW2L[0], w2buf[:],
                         W2f[ld * 128:(ld + 1) * 128, :])
        W("vector", sW2L[0], mLW2[ld])
        mRed2[ld] = EM("vector", lambda e, w2buf=w2buf, ld=ld: e.reduce_sum(
            parts[:, 16 + ld:17 + ld], w2buf[:], AX.X,
            apply_absolute_value=True), inc=sV)
    mP1a = p.now(sV)
    mThrS1 = tern_threshold(0, 8, 0, mP1a, me="gpsimd", ms=sG,
                            ss_wait=p.now(sS))[1]
    mP1b = p.now(sV)
    mThrS2 = tern_threshold(16, 8, 1, mP1b, me="gpsimd", ms=sG,
                            ss_wait=mThrS1)[1]

    bank1 = [PPbig[:, 0:512], PPbig[:, 512:1024], PPm[:, 0:512]]
    bank2 = [PPbig[:, 1024:1536], PPbig[:, 1536:2048], PPn[:, 0:512]]
    order1 = [6, 7, 0, 1, 2, 3, 4, 5]
    mTrG, mQG, mTrC = {}, {}, {}
    mTr2G, mQ2G, mTr2C = {}, {}, {}
    gi = 0
    gj = 0
    for i in range(8):
        ld = order1[i]
        buf = w1b[i % 2]
        if i >= 2:
            W("sync", sT, mTrC[order1[i - 2]])
            mLh = p.dma("sync", sLxb[i % 2],
                        w1b_dst(buf)[:, :, 0:512], w1f_src(ld)[:, :, 0:512])
            mL = p.dma("sync", sLx[i % 2],
                       w1b_dst(buf)[:, :, 512:1024], w1f_src(ld)[:, :, 512:1024])
            hsem = sLxb[i % 2]
        else:
            mLh = mL = mLd1[ld]
            hsem = sLx[i % 2]
        for t in range(DT):
            if t < 4:
                W("tensor", hsem, mLh)
            else:
                W("tensor", sLx[i % 2], mL)
            W("tensor", sG, mPrepG)
            W("tensor", sS, mThrS2)
            if gi >= 3:
                W("tensor", sG, mQG[gi - 3])
            pb1 = bank1[gi % 3]
            for a in range(4):
                EM("tensor", lambda e, t=t, a=a, pb1=pb1, buf=buf: e.transpose(
                    pb1[:, a * 128:(a + 1) * 128],
                    buf[:, a * E + t * 128: a * E + t * 128 + 128],
                    ident_f[:, :]), inc=(sT if a == 3 else None))
            mTrG[(ld, t)] = p.now(sT)
            rp = rnd[gi % 3]
            W("scalar", sT, mTrG[(ld, t)])
            W("scalar", sS, mThrS1)
            if gi >= 3:
                W("scalar", sG, mQG[gi - 3])
            EM("scalar", lambda e, pb1=pb1, rp=rp: e.activation(
                rp[0][:], pb1[:, 0:512], AF.Sign,
                bias=thr_sb[:, 1:2]), inc=sS)
            mSb = EM("scalar", lambda e, pb1=pb1, rp=rp: e.activation(
                rp[1][:], pb1[:, 0:512], AF.Sign,
                bias=thr_sb[:, 0:1]), inc=sS)
            W("gpsimd", sS, mSb)
            mQG[gi] = EM("gpsimd", lambda e, ld=ld, t=t, rp=rp: e.tensor_add(
                W1fT[:, t * FF + ld * 512: t * FF + ld * 512 + 512],
                rp[0][:], rp[1][:]), inc=sG)
            gi += 1
        mTrC[ld] = p.now(sT)
        ld2 = order1[i]
        buf2 = w2b if i % 2 == 0 else w2c
        if i >= 2:
            W("gpsimd", sT, mTr2C[order1[i - 2]])
            mL2h = p.dma("gpsimd", sW2L[2], buf2[:, 0:2048],
                         W2f[ld2 * 128:(ld2 + 1) * 128, 0:2048])
            mL2 = p.dma("gpsimd", sW2L[1], buf2[:, 2048:4096],
                        W2f[ld2 * 128:(ld2 + 1) * 128, 2048:4096])
            l2sem, l2hsem = sW2L[1], sW2L[2]
        else:
            mL2h = mL2 = mLW2[ld2]
            l2sem = l2hsem = sW2L[0]
        for g4 in range(FT // 4):
            if g4 < 4:
                W("tensor", l2hsem, mL2h)
            else:
                W("tensor", l2sem, mL2)
            W("tensor", sG, mPrepG)
            W("tensor", sS, mThrS2)
            if gj >= 3:
                W("tensor", sV, mQ2G[gj - 3])
            pb2 = bank2[gj % 3]
            for ff in range(4):
                f = g4 * 4 + ff
                EM("tensor", lambda e, f=f, ff=ff, pb2=pb2, buf2=buf2:
                   e.transpose(
                    pb2[:, ff * 128:(ff + 1) * 128],
                    buf2[:, f * 128:(f + 1) * 128],
                    ident_f[:, :]), inc=(sT if ff == 3 else None))
            mTr2G[gj] = p.now(sT)
            rp = rnd[3 + gj % 3]
            W("scalar", sT, mTr2G[gj])
            W("scalar", sS, mThrS2)
            if gj >= 3:
                W("scalar", sV, mQ2G[gj - 3])
            EM("scalar", lambda e, pb2=pb2, rp=rp: e.activation(
                rp[0][:], pb2[:, 0:512], AF.Sign,
                bias=thr_sb[:, 3:4]), inc=sS)
            mSb2 = EM("scalar", lambda e, pb2=pb2, rp=rp: e.activation(
                rp[1][:], pb2[:, 0:512], AF.Sign,
                bias=thr_sb[:, 2:3]), inc=sS)
            W("vector", sS, mSb2)
            for ff in range(4):
                f = g4 * 4 + ff
                EM("vector", lambda e, f=f, ld2=ld2, ff=ff, rp=rp:
                   e.tensor_add(
                    W2fT[:, f * E + ld2 * 128: f * E + ld2 * 128 + 128],
                    rp[0][:, ff * 128:(ff + 1) * 128],
                    rp[1][:, ff * 128:(ff + 1) * 128]), inc=(
                        sV if ff == 3 else None))
            mQ2G[gj] = p.now(sV)
            gj += 1
        mTr2C[ld2] = p.now(sT)
    mW1Done = p.now(sG)
    mW2Done = p.now(sV)
    mW1TrAll = p.now(sT)
    mW2TrAll = p.now(sT)

    W("gpsimd", sG, mW1Done)
    HW = DT * FF // 2
    p.dma("gpsimd", sWst, w1t_hbm[:, 0:HW], W1fT[:, 0:HW])
    mW1img = p.dma("gpsimd", sWst, w1t_hbm[:, HW:2 * HW], W1fT[:, HW:2 * HW])
    W("gpsimd", sV, mW2Done)
    p.dma("gpsimd", sWst2, w2t_hbm[:, 0:HW], W2fT[:, 0:HW])
    mW2img = p.dma("gpsimd", sWst2, w2t_hbm[:, HW:2 * HW], W2fT[:, HW:2 * HW])

    W("sync", sT, mW2TrAll)
    for eng_ in ("scalar", "vector"):
        W(eng_, sT, mW2TrAll)
        W(eng_, sV, mW2Done)
        W(eng_, sG, mW1Done)
        W(eng_, sS, p.now(sS))
    W("tensor", sG, mW1Done)
    W("tensor", sS, mThrS1)
    W("tensor", sS, mThrS2)

    def ffn_phase(x_in_dram, x_out_dram, pre_mm2=None):
        mX, mXb, mStats, mMV, mRstd, mBcast, mNorm = {}, {}, {}, {}, {}, {}, {}
        mGelu, mMM1, mMM2, mE, mAdd, mEpi, mStore = {}, {}, {}, {}, {}, {}, {}

        def load(c):
            if c >= NCH:
                return
            if c >= 3:
                W("sync", sV, mEpi[c - 3])
            mX[c] = p.dma("sync", sLx[c % 3],
                          x0c[c % 3][:].rearrange("p (t s) -> p t s", t=DT),
                          xchunk_ap(x_in_dram, c * SC, SC))

        def ln_front(c):
            if c >= NCH:
                return
            W("scalar", sLx[c % 3], mX[c])
            if c >= 2:
                W("scalar", sT, mStats[c - 2])
                W("scalar", sV, mNorm[c - 2])
            EM("scalar", lambda e, c=c: e.copy(xbc[c % 2][:], x0c[c % 3][:]), inc=sS)
            if c >= 1:
                W("scalar", sT, mStats[c - 1])
            W("scalar", sS)
            mXb[c] = EM("scalar", lambda e, c=c: e.square(xsqc[:], xbc[c % 2][:]), inc=sS)
            W("tensor", sS, mXb[c])
            if c >= 1:
                W("tensor", sS, mMV[c - 1])
            for t in range(DT):
                EM("tensor", lambda e, c=c, t=t: e.matmul(
                    PPm[0:1, 0:SC], ones_bf[:, 0:1],
                    xbc[c % 2][:, t * SC:(t + 1) * SC],
                    start=(t == 0), stop=(t == DT - 1)),
                    inc=(sT if t == DT - 1 else None))
            for t in range(DT):
                EM("tensor", lambda e, t=t: e.matmul(
                    PPm[0:1, 512:512 + SC], ones_bf[:, 0:1],
                    xsqc[:, t * SC:(t + 1) * SC],
                    start=(t == 0), stop=(t == DT - 1)),
                    inc=(sT if t == DT - 1 else None))
            mStats[c] = p.now(sT)

        def ln_mid(c):
            if c >= NCH:
                return
            W("scalar", sT, mStats[c])
            if c >= 1:
                W("scalar", sT, mBcast[c - 1])
            EM("scalar", lambda e: e.activation(
                stat_m[0:1, 0:SC], PPm[0:1, 0:SC], AF.Identity,
                scale=cst[0:1, 0:1]), inc=sS)
            W("scalar", sS)
            EM("scalar", lambda e: e.square(stat_m2[0:1, 0:SC], stat_m[0:1, 0:SC]),
               inc=sS)
            EM("scalar", lambda e: e.activation(
                stat_v[0:1, 0:SC], PPm[0:1, 512:512 + SC], AF.Identity,
                scale=cst[0:1, 0:1]), inc=sS)
            mMV[c] = p.now(sS)
            W("vector", sS, mMV[c])
            if c >= 1:
                W("vector", sT, mBcast[c - 1])
            EM("vector", lambda e: e.tensor_sub(
                stat_v[0:1, 0:SC], stat_v[0:1, 0:SC], stat_m2[0:1, 0:SC]), inc=sV)
            W("scalar", sV)
            EM("scalar", lambda e: e.activation(
                stat_sq[0:1, 0:SC], stat_v[0:1, 0:SC], AF.Sqrt,
                bias=cst[0:1, 4:5]), inc=sS)
            W("vector", sS)
            mRstd[c] = EM("vector", lambda e: e.reciprocal(
                stat_r[0:1, 0:SC], stat_sq[0:1, 0:SC]), inc=sV)

        def ln_bcast(c):
            if c >= NCH:
                return
            W("tensor", sV, mRstd[c])
            if c >= 1:
                W("tensor", sV, mNorm[c - 1])
            EM("tensor", lambda e: e.matmul(
                PPn[:, 0:SC], ones_f[0:1, :], stat_m[0:1, 0:SC],
                start=True, stop=True), inc=sT)
            EM("tensor", lambda e: e.matmul(
                PPn[:, 512:512 + SC], ones_f[0:1, :], stat_r[0:1, 0:SC],
                start=True, stop=True), inc=sT)
            mBcast[c] = p.now(sT)

        def ln_norm(c):
            if c >= NCH:
                return
            W("vector", sT, mBcast[c])
            if c >= 2:
                W("vector", sS, mMM1[c - 2])
            mSubT, mMulT = {}, {}
            t = 0
            for gsz in (3, 3, 2):
                for i in range(gsz):
                    tt_ = t + i
                    if tt_ >= 3:
                        W("vector", sV, mMulT[tt_ - 3])
                    mSubT[tt_] = EM("vector", lambda e, c=c, tt_=tt_, i2=tt_ % 3:
                                    e.tensor_sub(
                        ntmp[i2][:], xbc[c % 2][:, tt_ * SC:(tt_ + 1) * SC],
                        PPn[:, 0:SC]), inc=sV)
                for i in range(gsz):
                    tt_ = t + i
                    W("vector", sV, mSubT[tt_])
                    mMulT[tt_] = EM("vector", lambda e, c=c, tt_=tt_, i2=tt_ % 3:
                                    e.tensor_mul(
                        yc[c % 2][:, tt_ * SC:(tt_ + 1) * SC], ntmp[i2][:],
                        PPn[:, 512:512 + SC]), inc=sV)
                t += gsz
            mNorm[c] = p.now(sV)

        def mm1(c):
            W("tensor", sV, mNorm[c])
            for f in range(FT):
                pb = PPbig[:, (f % 2) * 512:(f % 2) * 512 + SC]
                if f >= 2:
                    W("tensor", sS, mGelu[(c, f - 2)])
                elif c >= 1:
                    W("tensor", sS, mGelu[(c - 1, FT - 2 + f)])
                for t in range(DT):
                    EM("tensor", lambda e, c=c, f=f, t=t, pb=pb: e.matmul(
                        pb, W1fT[:, t * FF + f * 128: t * FF + f * 128 + 128],
                        yc[c % 2][:, t * SC:(t + 1) * SC],
                        start=(t == 0), stop=(t == DT - 1)),
                        inc=(sT if t == DT - 1 else None))
                W("scalar", sT)
                if c >= 1 and f == 0:
                    W("scalar", sT, mMM2[c - 1])
                mGelu[(c, f)] = EM("scalar", lambda e, f=f, pb=pb: e.activation(
                    hc[:, f * SC:(f + 1) * SC], pb, AF.Gelu,
                    bias=b1f_sb[:, f:f + 1], scale=cst[:, 3:4]), inc=sS)
            mMM1[c] = p.now(sS)

        def mm2_epi(c):
            if c == 0 and pre_mm2 is not None:
                pre_mm2()
            for dch in range(DT):
                pb = PPbig[:, 1024 + (dch % 2) * 512:1024 + (dch % 2) * 512 + SC]
                if dch >= 2:
                    W("tensor", sS, mE[(c, dch - 2)])
                elif c >= 1:
                    W("tensor", sS, mE[(c - 1, DT - 2 + dch)])
                for f in range(FT):
                    W("tensor", sS, mGelu[(c, f)])
                    EM("tensor", lambda e, c=c, f=f, dch=dch, pb=pb: e.matmul(
                        pb, W2fT[:, f * E + dch * 128: f * E + dch * 128 + 128],
                        hc[:, f * SC:(f + 1) * SC],
                        start=(f == 0), stop=(f == FT - 1)),
                        inc=(sT if f == FT - 1 else None))
                W("scalar", sT)
                if dch >= 2:
                    W("scalar", sV, mAdd[(c, dch - 2)])
                elif c >= 1:
                    W("scalar", sV, mAdd[(c - 1, DT - 2 + dch)])
                mE[(c, dch)] = EM("scalar", lambda e, dch=dch, pb=pb: e.activation(
                    ec[dch % 2][:], pb, AF.Identity,
                    bias=hb2f_sb[:, dch:dch + 1], scale=cst[:, 6:7]), inc=sS)
                W("vector", sS, mE[(c, dch)])
                if dch == 0 and c >= 1:
                    W("vector", sSt[(c - 1) % 2], mStore[c - 1])
                mAdd[(c, dch)] = EM("vector", lambda e, c=c, dch=dch: e.tensor_add(
                    x1oc[:, dch * SC:(dch + 1) * SC], ec[dch % 2][:],
                    x0c[c % 3][:, dch * SC:(dch + 1) * SC]), inc=sV)
            mMM2[c] = p.now(sT)
            mEpi[c] = p.now(sV)
            W("gpsimd", sV, mEpi[c])
            mStore[c] = p.dma("gpsimd", sSt[c % 2],
                              xchunk_ap(x_out_dram, c * SC, SC),
                              x1oc[:].rearrange("p (t s) -> p t s", t=DT))

        load(0); load(1); load(2)
        ln_front(0); ln_mid(0); ln_bcast(0); ln_norm(0)
        ln_front(1); ln_mid(1)
        for c in range(NCH):
            mm1(c)
            ln_bcast(c + 1); ln_norm(c + 1)
            ln_front(c + 2); ln_mid(c + 2)
            mm2_epi(c)
            load(c + 3)

    def _w2ft_ready():
        W("tensor", sV, mW2Done)
    ffn_phase(xT_in, x1_hbm, pre_mm2=_w2ft_ready)
    p.barrier()

    EM("gpsimd", lambda e: e.memset(mask_sb[:], 0.0), inc=sG)
    W("gpsimd", sG)
    for o_i, o in enumerate((0, 128, 256, 384)):
        EM("gpsimd", lambda e, o_i=o_i, o=o: e.affine_select(
            out=mask_sb[:, o_i * 512:(o_i + 1) * 512],
            in_=mask_sb[:, o_i * 512:(o_i + 1) * 512],
            compare_op=ALU.is_ge, fill=1.0, base=o - 1,
            pattern=[[-1, 512]], channel_multiplier=1), inc=sG)
    mMasks = p.now(sG)
    mBv = p.dma("sync", sLa[0],
                bvrep_sb[0:1, :].rearrange("p (h c) -> p h c", h=H),
                b1a.rearrange("(h c) -> h c", c=192)[:, 128:192])
    W("tensor", sLa[0], mBv)
    EM("tensor", lambda e: e.matmul(
        PPm[:, 0:512], ones_f[0:1, :], bvrep_sb[0:1, 0:512],
        start=True, stop=True), inc=sT)
    EM("tensor", lambda e: e.matmul(
        PPm[:, 512:1024], ones_f[0:1, :], bvrep_sb[0:1, 512:1024],
        start=True, stop=True), inc=sT)
    W("scalar", sT)
    mBvRep = EM("scalar", lambda e: e.activation(
        bvrep_sb[:, :], PPm[:, 0:1024], AF.Identity), inc=sS)

    mX2, mXb2, mSt2, mMV2, mRs2, mBc2, mN2 = {}, {}, {}, {}, {}, {}, {}

    def ln2_load(c):
        if c >= NCH:
            return
        if c >= 2:
            W("sync", sS, mXb2[c - 2])
        mX2[c] = p.dma("sync", sLx[c % 2],
                       x1c2[c % 2][:].rearrange("p (t s) -> p t s", t=DT),
                       xchunk_ap(x1_hbm, c * SC, SC))

    def ln2_front(c):
        if c >= NCH:
            return
        W("scalar", sLx[c % 2], mX2[c])
        if c >= 2:
            W("scalar", sT, mSt2[c - 2])
            W("scalar", sV, mN2[c - 2])
        EM("scalar", lambda e, c=c: e.copy(xbc2[c % 2][:], x1c2[c % 2][:]), inc=sS)
        if c >= 1:
            W("scalar", sT, mSt2[c - 1])
        W("scalar", sS)
        mXb2[c] = EM("scalar", lambda e, c=c: e.square(xsqc2[:], xbc2[c % 2][:]), inc=sS)
        W("tensor", sS, mXb2[c])
        stp = PPm if c % 2 == 0 else PPn
        if c >= 2:
            W("tensor", sS, mMV2[c - 2])
        for t in range(DT):
            EM("tensor", lambda e, c=c, t=t, stp=stp: e.matmul(
                stp[0:1, 0:SC], ones_bf[:, 0:1],
                xbc2[c % 2][:, t * SC:(t + 1) * SC],
                start=(t == 0), stop=(t == DT - 1)), inc=(sT if t == DT - 1 else None))
        for t in range(DT):
            EM("tensor", lambda e, t=t, stp=stp: e.matmul(
                stp[0:1, 512:512 + SC], ones_bf[:, 0:1],
                xsqc2[:, t * SC:(t + 1) * SC],
                start=(t == 0), stop=(t == DT - 1)), inc=(sT if t == DT - 1 else None))
        mSt2[c] = p.now(sT)

    def ln2_mid(c):
        if c >= NCH:
            return
        W("scalar", sT, mSt2[c])
        stp = PPm if c % 2 == 0 else PPn
        if c >= 1:
            W("scalar", sT, mBc2[c - 1])
        EM("scalar", lambda e, stp=stp: e.activation(
            stat_m[0:1, 0:SC], stp[0:1, 0:SC], AF.Identity, scale=cst[0:1, 0:1]), inc=sS)
        W("scalar", sS)
        EM("scalar", lambda e: e.square(stat_m2[0:1, 0:SC], stat_m[0:1, 0:SC]), inc=sS)
        EM("scalar", lambda e, stp=stp: e.activation(
            stat_v[0:1, 0:SC], stp[0:1, 512:512 + SC], AF.Identity,
            scale=cst[0:1, 0:1]), inc=sS)
        mMV2[c] = p.now(sS)
        W("vector", sS, mMV2[c])
        if c >= 1:
            W("vector", sT, mBc2[c - 1])
        EM("vector", lambda e: e.tensor_sub(
            stat_v[0:1, 0:SC], stat_v[0:1, 0:SC], stat_m2[0:1, 0:SC]), inc=sV)
        W("scalar", sV)
        EM("scalar", lambda e: e.activation(
            stat_sq[0:1, 0:SC], stat_v[0:1, 0:SC], AF.Sqrt, bias=cst[0:1, 4:5]), inc=sS)
        W("vector", sS)
        mRs2[c] = EM("vector", lambda e: e.reciprocal(
            stat_r[0:1, 0:SC], stat_sq[0:1, 0:SC]), inc=sV)

    def ln2_back(c):
        if c >= NCH:
            return
        W("tensor", sV, mRs2[c])
        bq = 0 if c % 2 == 0 else 1024
        if c >= 2:
            W("tensor", sV, mN2[c - 2])
        EM("tensor", lambda e, bq=bq: e.matmul(
            PPbig[:, bq:bq + SC], ones_f[0:1, :], stat_m[0:1, 0:SC],
            start=True, stop=True), inc=sT)
        EM("tensor", lambda e, bq=bq: e.matmul(
            PPbig[:, bq + 512:bq + 512 + SC], ones_f[0:1, :], stat_r[0:1, 0:SC],
            start=True, stop=True), inc=sT)
        mBc2[c] = p.now(sT)
        W("vector", sT, mBc2[c])
        mSubT, mMulT = {}, {}
        t = 0
        for gsz in (3, 3, 2):
            for i in range(gsz):
                tt_ = t + i
                if tt_ >= 3:
                    W("vector", sV, mMulT[tt_ - 3])
                mSubT[tt_] = EM("vector", lambda e, c=c, tt_=tt_, i2=tt_ % 3, bq=bq:
                                e.tensor_sub(
                    ntmp[i2][:], xbc2[c % 2][:, tt_ * SC:(tt_ + 1) * SC],
                    PPbig[:, bq:bq + SC]), inc=sV)
            for i in range(gsz):
                tt_ = t + i
                W("vector", sV, mSubT[tt_])
                mMulT[tt_] = EM("vector", lambda e, c=c, tt_=tt_, i2=tt_ % 3, bq=bq:
                                e.tensor_mul(
                    y2T[:, tt_ * S + c * SC: tt_ * S + c * SC + SC],
                    ntmp[i2][:], PPbig[:, bq + 512:bq + 512 + SC]), inc=sV)
            t += gsz
        mN2[c] = p.now(sV)

    ln2_load(0); ln2_load(1)
    ln2_front(0); ln2_mid(0)
    for c in range(NCH):
        ln2_load(c + 2)
        ln2_back(c)
        ln2_front(c + 1); ln2_mid(c + 1)
    p.barrier()

    NG = H // 4
    v4_sb = [v4a, v4b]
    mWqk, mWv, mVG, mQKdone, mDup = {}, {}, {}, {}, {}
    mScAll, mAvDone, mAstStore = {}, {}, {}
    mEvacPair, mZstore, mRecip, mRzQ = {}, {}, {}, {}
    mNormBc, mNormMult = {}, {}
    lastAvHalf = {}
    lastExpOn = {}
    pqLastAv = {}
    lastFillEvac = [None, None]
    lastNormBc = [None, None]
    lastNormMult = [None, None]
    lastZst = [None, None]
    fillq = []
    normq = []

    for vb in range(2):
        EM("gpsimd", lambda e, vb=vb: e.memset(
            v4_sb[vb][:].rearrange("p (ss j c) -> p ss j c", j=4, c=65)[:, :, :, 64:65],
            1.0), inc=sG)
    mOnesV = p.now(sG)

    def wqk_load(h):
        if h >= H:
            return
        if h >= 2 and (h - 2) in mQKdone:
            W("gpsimd", sT, mQKdone[h - 2][1])
        src_ = W1a.rearrange("(t p) f -> p t f", p=128)[:, :, h * 192: h * 192 + 128]
        mWqk[h] = p.dma("gpsimd", sGw[h % 2],
                        wqk_sb[h % 2][:].rearrange("p (t f) -> p t f", t=DT), src_)

    def wv_load(g):
        if g >= NG:
            return
        if g >= 1:
            W("gpsimd", sT, mVG[g - 1][1])
        for j in range(4):
            hh = g * 4 + j
            src_ = W1a.rearrange("(t p) f -> p t f", p=128)[
                :, :, hh * 192 + 128: hh * 192 + 192]
            dst = wv4_sb[:].rearrange("p (t f) -> p t f", t=DT)[:, :, j * 64:(j + 1) * 64]
            mWv[g] = p.dma("gpsimd", sGw[g % 2], dst, src_)

    def wait_fill_bank(eng):
        kind, m = lastFillEvac
        if m is not None:
            W(eng, sS if kind == "S" else sV, m)

    def fill_v(g, ss):
        W("tensor", sGw[g % 2], mWv[g])
        W("tensor", sG, mOnesV)
        if g >= 2:
            W("tensor", sT, mScAll.get((g - 2) * 4 + 3, 0) or 0)
        wait_fill_bank("tensor")
        for t in range(DT):
            EM("tensor", lambda e, ss=ss, t=t: e.matmul(
                PPm[:, 512:768],
                y2T[:, t * S + ss * 128: t * S + ss * 128 + 128],
                wv4_sb[:, t * 256:(t + 1) * 256],
                start=(t == 0), stop=(t == DT - 1)),
                inc=(sT if t == DT - 1 else None))
        mV = p.now(sT)
        W("vector", sT, mV)
        W("vector", sS, mBvRep)
        mE_ = EM("vector", lambda e, g=g, ss=ss: e.tensor_add(
            v4_sb[g % 2][:].rearrange("p (s2 j c) -> p s2 j c", j=4, c=65)[
                :, ss, :, 0:64],
            PPm[:, 512:768], bvrep_sb[:, g * 256:(g + 1) * 256]), inc=sV)
        lastFillEvac[0], lastFillEvac[1] = "V", mE_
        if ss == NSS - 1:
            mVG[g] = (p.now(sV), p.now(sT))

    def fill_qk(h, qs):
        W("tensor", sGw[h % 2], mWqk[h])
        if h >= 2 and (h - 2) in mScAll:
            W("tensor", sT, mScAll[h - 2])
        wait_fill_bank("tensor")
        for t in range(DT):
            EM("tensor", lambda e, h=h, qs=qs, t=t: e.matmul(
                PPm[0:128, 512:1024],
                wqk_sb[h % 2][:, t * 128: t * 128 + 128],
                y2T[:, t * S + qs * 512: t * S + qs * 512 + 512],
                start=(t == 0), stop=(t == DT - 1)),
                inc=(sT if t == DT - 1 else None))
        W("vector", sT)
        mC = EM("vector", lambda e, h=h, qs=qs: e.tensor_scalar(
            qk_sb[h % 2][0:128, qs * 512:(qs + 1) * 512], PPm[0:128, 512:1024],
            bqk_sb[:, h:h + 1], None, ALU.add), inc=sV)
        lastFillEvac[0], lastFillEvac[1] = "V", mC
        halves = []
        if qs == max(NQ // 2 - 1, 0):
            halves.append(0)
        if qs == NQ - 1:
            halves.append(1)
        for half in halves:
            c0, c1 = half * (S // 2), (half + 1) * (S // 2)
            sdp = sLa[half]
            W("sync", sV, mC)
            if h >= 2 and (h - 2) in mScAll:
                W("sync", sT, mScAll[h - 2])
            p.dma("sync", sdp, kq_sb[h % 2][64:128, c0:c1],
                  qk_sb[h % 2][0:64, c0:c1])
            mDup[(h, half)] = p.dma("sync", sdp, kq_sb[h % 2][0:64, c0:c1],
                                    qk_sb[h % 2][64:128, c0:c1])
        if qs == NQ - 1:
            mQKdone[h] = (p.now(sV), p.now(sT))

    def enqueue_head(h, with_loads=False):
        if h >= H:
            return
        if with_loads:
            wqk_load(h)
            if h % 4 == 0:
                wv_load(h // 4)
        if h % 4 == 0:
            for ss in range(NSS):
                fillq.append(lambda g=h // 4, ss=ss: fill_v(g, ss))
        for qs in range(NQ):
            fillq.append(lambda h=h, qs=qs: fill_qk(h, qs))

    def pop_fill():
        if fillq:
            fillq.pop(0)()

    def drain_fill():
        while fillq:
            fillq.pop(0)()

    def norm_job(h, qc):
        kk_ = h * NQ + qc
        slot = kk_ % 2
        r0 = slot * 64
        W("tensor", sRzQ[0], mRzQ[h])
        if lastNormMult[slot] is not None:
            W("tensor", sV, lastNormMult[slot])
        mNormBc[(h, qc)] = EM("tensor", lambda e, qc=qc, r0=r0: e.matmul(
            PPm[r0:r0 + 64, 0:512], ones_bf[0:1, 0:64],
            rzq[0:1, qc * 512:(qc + 1) * 512],
            start=True, stop=True), inc=sT)
        lastNormBc[0] = mNormBc[(h, qc)]
        W("vector", sT, mNormBc[(h, qc)])
        if kk_ >= 2:
            W("vector", sSA[kk_ % 2], mAstStore[kk_ - 2])
        mNormMult[(h, qc)] = EM("vector", lambda e, h=h, qc=qc, kk_=kk_, r0=r0:
                                e.tensor_mul(
            ast_sb[kk_ % 2][0:64, 0:512],
            astg[h % 2][:, qc * 512:(qc + 1) * 512],
            PPm[r0:r0 + 64, 0:512]), inc=sV)
        lastNormMult[slot] = mNormMult[(h, qc)]
        W("sync", sV, mNormMult[(h, qc)])
        mAstStore[kk_] = p.dma("sync", sSA[kk_ % 2],
                               a_hbm[h * 64:(h + 1) * 64,
                                     qc * 512:(qc + 1) * 512],
                               ast_sb[kk_ % 2][0:64, 0:512])

    def pop_norm(n):
        for _ in range(n):
            if normq:
                normq.pop(0)()

    def attn_head(h):
        g = h // 4
        j = h % 4
        qksb, kqsb = qk_sb[h % 2], kq_sb[h % 2]
        pairs = []
        for qc in range(NQ):
            npair = 2 * (qc + 1)
            for pr in range(npair):
                pairs.append((qc, pr, npair))
        NPH = len(pairs)
        mExpP, mMskP, prPq = {}, {}, {}

        def epilogue(qc):
            avsl = (qc % 2) * 512
            W("vector", sT, mAvDone[(h, qc)])
            EM("vector", lambda e, h=h, qc=qc, avsl=avsl: e.tensor_copy(
                astg[h % 2][:, qc * 512:(qc + 1) * 512],
                PPn[0:64, avsl:avsl + 512]), inc=sV)
            if lastZst[0] is not None:
                W("vector", sZst[0], lastZst[0])
            EM("vector", lambda e, avsl=avsl: e.tensor_copy(
                zst[64:65, 0:512], PPn[64:65, avsl:avsl + 512]), inc=sV)
            mEvacPair[(h, qc)] = p.now(sV)
            W("gpsimd", sV, mEvacPair[(h, qc)])
            if qc == 0 and h >= 1:
                W("gpsimd", sV, mRecip[h - 1])
            mZstore[(h, qc)] = p.dma(
                "gpsimd", sZst[0], zc2[qc * 16:(qc + 1) * 16, :],
                zst[64:65, 0:512].rearrange("p (a s) -> p a s", a=16))
            lastZst[0] = mZstore[(h, qc)]

        def stage_a(i):
            qc, pr, npair = pairs[i]
            kc0 = 2 * pr
            sc0 = (i % 2) * 1024
            le = lastExpOn.get(i % 2)
            if le is not None:
                W("tensor", sS, le)
            W("tensor", sV, mQKdone[h][0])
            W("tensor", sT, mQKdone[h][1])
            W("tensor", sLa[0], mDup[(h, 0)])
            if (qc + 1) * 512 > S // 2 or (kc0 + 2) * 128 > S // 2:
                W("tensor", sLa[1], mDup[(h, 1)])
            W("tensor", sV, mVG[g][0])
            W("tensor", sT, mVG[g][1])
            for rt in range(2):
                kc = kc0 + rt
                r0 = rt * 64
                ktile = kqsb if rt == 0 else qksb
                qtile = qksb if rt == 0 else kqsb
                EM("tensor", lambda e, qc=qc, kc=kc, rt=rt, r0=r0, sc0=sc0,
                   ktile=ktile, qtile=qtile: e.matmul(
                    PPbig[0:128, sc0 + rt * 512: sc0 + rt * 512 + 512],
                    ktile[r0:r0 + 64, kc * 128:(kc + 1) * 128],
                    qtile[r0:r0 + 64, qc * 512:(qc + 1) * 512],
                    start=True, stop=True), inc=(sT if rt == 1 else None))
            mScP = p.now(sT)
            pq_i = i % 4
            prPq[i] = pq_i
            W("scalar", sT, mScP)
            la = pqLastAv.get(pq_i)
            if la is not None:
                W("scalar", sT, la)
            mExpP[i] = EM("scalar", lambda e, pq_i=pq_i, sc0=sc0: e.activation(
                pq_sb[pq_i][:, 0:1024], PPbig[:, sc0:sc0 + 1024], AF.Exp,
                bias=cst[:, 1:2], scale=cst[:, 2:3]), inc=sS)
            lastExpOn[i % 2] = mExpP[i]
            dpr = pr - (npair - 2)
            if dpr >= 0:
                W("vector", sS, mExpP[i])
                mMskP[i] = EM("vector", lambda e, pq_i=pq_i, dpr=dpr: e.tensor_mul(
                    pq_sb[pq_i][:, 0:1024], pq_sb[pq_i][:, 0:1024],
                    mask_sb[:, dpr * 1024:(dpr + 1) * 1024]), inc=sV)

        def stage_b(i):
            qc, pr, npair = pairs[i]
            kc0 = 2 * pr
            nk = 2 * npair
            avsl = (qc % 2) * 512
            pq_i = prPq[i]
            if pr == 0:
                prev = lastAvHalf.get(qc % 2)
                if prev is not None:
                    W("tensor", sV, mEvacPair[prev])
                lastAvHalf[qc % 2] = (h, qc)
            if i in mMskP:
                W("tensor", sV, mMskP[i])
            else:
                W("tensor", sS, mExpP[i])
            for kk in range(2):
                kc = kc0 + kk
                EM("tensor", lambda e, kc=kc, kk=kk, pq_i=pq_i, j=j, avsl=avsl,
                   st=(kc0 + kk == 0), sp=(kc0 + kk == nk - 1): e.matmul(
                    PPn[0:65, avsl:avsl + 512],
                    v4_sb[g % 2][:, kc * 4 * 65 + j * 65: kc * 4 * 65 + j * 65 + 65],
                    pq_sb[pq_i][:, kk * 512:(kk + 1) * 512],
                    start=st, stop=sp), inc=(sT if kk == 1 else None))
            mAvP = p.now(sT)
            pqLastAv[pq_i] = mAvP
            if pr == npair - 1:
                mAvDone[(h, qc)] = mAvP
                epilogue(qc)

        for i in range(NPH):
            qc, pr, npair = pairs[i]
            if pr == 0 and (qc in (3, 5, 6, 7) or NQ == 1):
                pop_norm(2)
            stage_a(i)
            if i >= 2:
                stage_b(i - 2)
            pop_fill()
        stage_b(NPH - 2)
        stage_b(NPH - 1)
        W("vector", sZst[0], mZstore[(h, NQ - 1)])
        if h >= 1:
            W("vector", sRzQ[0], mRzQ[h - 1])
        def _recip(e):
            with nc.allow_low_precision(reason="1/Z bf16 broadcast operand"):
                return e.reciprocal(rz2[0:16 * NQ, :], zc2[0:16 * NQ, :])
        mRecip[h] = EM("vector", _recip, inc=sV)
        W("gpsimd", sV, mRecip[h])
        if h >= 1 and (h - 1, NQ - 1) in mNormBc:
            W("gpsimd", sT, mNormBc[(h - 1, NQ - 1)])
        mRzQ[h] = p.dma("gpsimd", sRzQ[0],
                        rzq[0:1, :].rearrange("p (q s) -> p q s", q=16 * NQ),
                        rz2[0:16 * NQ, :])
        for qc in range(NQ):
            normq.append(lambda h=h, qc=qc: norm_job(h, qc))
        mScAll[h] = p.now(sT)

    enqueue_head(0, with_loads=True)
    drain_fill()
    enqueue_head(1, with_loads=True)
    mW1re = None
    for h in range(H):
        wqk_load(h + 2)
        if (h + 2) % 4 == 0:
            wv_load((h + 2) // 4)
        if h == H - 1 and DT * S * 2 >= DT * FF * 2:
            W("sync", sV, mQKdone[h][0])
            W("sync", sT, mQKdone[h][1])
            W("sync", sV, mVG[NG - 1][0])
            W("sync", sT, mVG[NG - 1][1])
            W("sync", sWst, mW1img)
            HW_ = DT * FF // 2
            p.dma("sync", sTr[0], W1fT[:, 0:HW_], w1t_hbm[:, 0:HW_])
            mW1re = p.dma("sync", sTr[0], W1fT[:, HW_:2 * HW_],
                          w1t_hbm[:, HW_:2 * HW_])
        attn_head(h)
        drain_fill()
        enqueue_head(h + 2)
    W("gpsimd", sT, mAvDone[(H - 1, NQ - 1)])
    mWp = p.dma("gpsimd", sGw[0],
                wp_sb[:].rearrange("p (t d) -> p t d", t=DT),
                Wp.rearrange("(t p) d -> p t d", p=128))
    while normq:
        normq.pop(0)()
    p.barrier()

    W("sync", sWst2, mW2img)
    HW4 = FT * E // 2
    p.dma("sync", sTr[1], W2fT[:, 0:HW4], w2t_hbm[:, 0:HW4])
    mW2re = p.dma("sync", sTr[1], W2fT[:, HW4:2 * HW4], w2t_hbm[:, HW4:2 * HW4])
    mA4p, mX14p = {}, {}
    W("sync", sV, p.now(sV))
    W("sync", sT, p.now(sT))
    W("sync", sSA[0], p.now(sSA[0]))
    W("sync", sSA[1], p.now(sSA[1]))
    for c_ in range(2):
        mA4p[c_] = p.dma("sync", sLa[c_ % 2],
                         a4c[c_ % 2][:].rearrange("p (t s) -> p t s", t=DT),
                         xchunk_ap(a_hbm, c_ * SC, SC))
        mX14p[c_] = p.dma("sync", sLx[c_ % 2],
                          x1c4[c_ % 2][:].rearrange("p (t s) -> p t s", t=DT),
                          xchunk_ap(x1_hbm, c_ * SC, SC))
    if mW1re is None:
        W("sync", sWst, mW1img)
        HW_ = DT * FF // 2
        p.dma("sync", sTr[0], W1fT[:, 0:HW_], w1t_hbm[:, 0:HW_])
        mW1re = p.dma("sync", sTr[0], W1fT[:, HW_:2 * HW_],
                      w1t_hbm[:, HW_:2 * HW_])
    mA4, mX14, mE4, mAddP, mMM4, mAdd4, mSt4 = {}, {}, {}, {}, {}, {}, {}
    for c in range(NCH):
        if c < 2:
            mA4[c], mX14[c] = mA4p[c], mX14p[c]
        else:
            W("sync", sT, mMM4[c - 2])
            mA4[c] = p.dma("sync", sLa[c % 2],
                           a4c[c % 2][:].rearrange("p (t s) -> p t s", t=DT),
                           xchunk_ap(a_hbm, c * SC, SC))
            W("sync", sV, mAdd4[c - 2])
            mX14[c] = p.dma("sync", sLx[c % 2],
                            x1c4[c % 2][:].rearrange("p (t s) -> p t s", t=DT),
                            xchunk_ap(x1_hbm, c * SC, SC))
        W("tensor", sLa[c % 2], mA4[c])
        W("tensor", sGw[0], mWp)
        for dch in range(DT):
            pb = PPm[:, (dch % 2) * 512:(dch % 2) * 512 + SC]
            if dch >= 2:
                W("tensor", sS, mE4[(c, dch - 2)])
            elif c >= 1:
                W("tensor", sS, mE4[(c - 1, DT - 2 + dch)])
            for t in range(DT):
                EM("tensor", lambda e, c=c, dch=dch, t=t, pb=pb: e.matmul(
                    pb, wp_sb[:, t * E + dch * 128: t * E + dch * 128 + 128],
                    a4c[c % 2][:, t * SC:(t + 1) * SC],
                    start=(t == 0), stop=(t == DT - 1)),
                    inc=(sT if t == DT - 1 else None))
            W("scalar", sT)
            if dch >= 2:
                W("scalar", sV, mAddP[(c, dch - 2)])
            elif c >= 1:
                W("scalar", sV, mAddP[(c - 1, DT - 2 + dch)])
            mE4[(c, dch)] = EM("scalar", lambda e, dch=dch, pb=pb: e.activation(
                e4c[dch % 2][:], pb, AF.Identity, bias=bp_sb[:, dch:dch + 1]), inc=sS)
            W("vector", sS, mE4[(c, dch)])
            W("vector", sLx[c % 2], mX14[c])
            if dch == 0 and c >= 2:
                W("vector", sSt[c % 2], mSt4[c - 2])
            mAddP[(c, dch)] = EM("vector", lambda e, c=c, dch=dch: e.tensor_add(
                x2oc[c % 2][:, dch * SC:(dch + 1) * SC], e4c[dch % 2][:],
                x1c4[c % 2][:, dch * SC:(dch + 1) * SC]), inc=sV)
        mMM4[c] = p.now(sT)
        mAdd4[c] = p.now(sV)
        W("gpsimd", sV, mAdd4[c])
        mSt4[c] = p.dma("gpsimd", sSt[c % 2],
                        xchunk_ap(x2_hbm, c * SC, SC),
                        x2oc[c % 2][:].rearrange("p (t s) -> p t s", t=DT))
    p.barrier()

    W("tensor", sTr[0], mW1re)

    def _p5_w2():
        W("tensor", sTr[1], mW2re)
    ffn_phase(x2_hbm, outT, pre_mm2=_p5_w2)

    for par in range(2):
        p.wait("gpsimd", sSt[par], p.now(sSt[par]))

    p.build()
    return nc


def kernel(L0_attn_x, L0_attn_W1, L0_attn_b1, L0_attn_Wp, L0_attn_bp,
           L0_ffn_W1, L0_ffn_b1, L0_ffn_W2, L0_ffn_b2):
    x = np.asarray(L0_attn_x, dtype=np.float32)
    B, S, _ = x.shape
    nc = build_nc(S)
    w = {
        "W1a": np.ascontiguousarray(np.asarray(L0_attn_W1, np.float32)),
        "b1a": np.ascontiguousarray(np.asarray(L0_attn_b1, np.float32)),
        "Wp": np.ascontiguousarray(np.asarray(L0_attn_Wp, np.float32)),
        "bp": np.ascontiguousarray(np.asarray(L0_attn_bp, np.float32)),
        "W1f": np.ascontiguousarray(np.asarray(L0_ffn_W1, np.float32)),
        "b1f": np.ascontiguousarray(np.asarray(L0_ffn_b1, np.float32)),
        "W2f": np.ascontiguousarray(np.asarray(L0_ffn_W2, np.float32)),
        "b2f": np.ascontiguousarray(np.asarray(L0_ffn_b2, np.float32)),
    }
    in_maps = []
    for b in range(B):
        m = dict(w)
        m["xT_in"] = np.ascontiguousarray(x[b].T)
        in_maps.append(m)
    res = run_bass_kernel_spmd(nc, in_maps, core_ids=list(range(B)))
    out = np.stack([np.asarray(res.results[b]["outT"]).T for b in range(B)])
    return np.ascontiguousarray(out)


# version 39 (final)
# speedup vs baseline: 1.1807x; 1.1807x over previous
import numpy as np
import ml_dtypes
import concourse.bass as bass
import concourse.mybir as mybir
from concourse.bass_utils import run_bass_kernel_spmd

F32 = mybir.dt.float32
BF16 = mybir.dt.bfloat16
AF = mybir.ActivationFunctionType
ALU = mybir.AluOpType
AX = mybir.AxisListType

ENGINES = ("sync", "scalar", "vector", "tensor", "gpsimd")

E = 1024
H = 16
FF = 4096
DT = 8
FT = 32
SC = 256
QCW = 512
KCW = 128


class Prog:

    def __init__(self, nc):
        self.nc = nc
        self.ops = {e: [] for e in ENGINES}
        self.count = {}
        self.waited = {}
        self.sems = {}

    def sem(self, name):
        h = self.nc.ctx.enter_context(self.nc.semaphore(name))
        self.count[name] = 0
        self.sems[name] = h
        return h

    def now(self, sem):
        return self.count[sem.name]

    def wait(self, eng, sem, val=None):
        if val is None:
            val = self.count[sem.name]
        assert val <= self.count[sem.name], (sem.name, val, self.count[sem.name])
        if val <= 0:
            return
        key = (eng, sem.name)
        if self.waited.get(key, 0) >= val:
            return
        self.waited[key] = val
        self.ops[eng].append(("wait", sem, val))

    def emit(self, eng, fn, inc=None, inc_n=1):
        if inc is not None:
            self.count[inc.name] += inc_n
        self.ops[eng].append(("op", fn, inc, inc_n))
        return self.count[inc.name] if inc is not None else None

    def dma(self, eng, sem, dst, src, **kw):
        self.wait(eng, sem, self.count[sem.name])
        return self.emit(eng, lambda e: e.dma_start(dst, src, **kw), inc=sem, inc_n=16)

    def barrier(self):
        for eng in ENGINES:
            for name, cnt in self.count.items():
                if cnt > 0:
                    key = (eng, name)
                    if self.waited.get(key, 0) < cnt:
                        self.waited[key] = cnt
                        self.ops[eng].append(("wait", self.sems[name], cnt))

    def build(self):
        nc = self.nc
        with nc.Block() as block:
            def mk(eng_name):
                def run(engine):
                    for item in self.ops[eng_name]:
                        if item[0] == "wait":
                            engine.wait_ge(item[1], item[2])
                        else:
                            _, fn, inc, inc_n = item
                            inst = fn(engine)
                            if inc is not None:
                                inst.then_inc(inc, inc_n)
                return run
            block.sync(mk("sync"))
            block.scalar(mk("scalar"))
            block.vector(mk("vector"))
            block.tensor(mk("tensor"))
            block.gpsimd(mk("gpsimd"))


def build_nc(S=4096):
    assert S % 512 == 0
    NCH = S // SC
    NQ = S // QCW
    NSS = S // KCW

    nc = bass.Bass("TRN2", target_bir_lowering=False)

    xT_in = nc.dram_tensor("xT_in", [E, S], F32, kind="ExternalInput").ap()
    W1a = nc.dram_tensor("W1a", [E, 3 * E], F32, kind="ExternalInput").ap()
    b1a = nc.dram_tensor("b1a", [3 * E], F32, kind="ExternalInput").ap()
    Wp = nc.dram_tensor("Wp", [E, E], F32, kind="ExternalInput").ap()
    bp = nc.dram_tensor("bp", [E], F32, kind="ExternalInput").ap()
    W1f = nc.dram_tensor("W1f", [FF, E], F32, kind="ExternalInput").ap()
    b1f = nc.dram_tensor("b1f", [FF], F32, kind="ExternalInput").ap()
    W2f = nc.dram_tensor("W2f", [E, FF], F32, kind="ExternalInput").ap()
    b2f = nc.dram_tensor("b2f", [E], F32, kind="ExternalInput").ap()
    outT = nc.dram_tensor("outT", [E, S], F32, kind="ExternalOutput").ap()

    x1_hbm = nc.dram_tensor("x1_hbm", [E, S], F32, kind="Internal").ap()
    x2_hbm = nc.dram_tensor("x2_hbm", [E, S], F32, kind="Internal").ap()
    a_hbm = nc.dram_tensor("a_hbm", [E, S], BF16, kind="Internal").ap()
    w1t_hbm = nc.dram_tensor("w1t_hbm", [128, DT * FF], BF16, kind="Internal").ap()
    w2t_hbm = nc.dram_tensor("w2t_hbm", [128, FT * E], BF16, kind="Internal").ap()

    def xchunk_ap(dram, c0, width):
        return dram.rearrange("(t p) s -> p t s", p=128)[:, :, c0:c0 + width]

    base = (nc.sbuf_base + 31) & ~31
    cap = nc.sbuf_base + nc.sbuf_bytes_remaining - base
    A_OFF = base
    B_OFF = base + 128 * 1024
    P_OFF = base + ((cap - 2560) & ~31)

    def at(name, shape, dt, off):
        t = nc.alloc_sbuf_tensor_at(name, shape, dt, offset=off)
        return t

    po = [P_OFF]

    def pat(name, shape, dt, size):
        t = at(name, shape, dt, po[0])
        po[0] += (size + 31) & ~31
        return t

    cst = pat("cst", [128, 8], F32, 32)
    ones_bf = pat("ones_bf", [128, 64], BF16, 128)
    ones_f = pat("ones_f", [128, 128], F32, 512)
    ident_f = pat("ident_f", [128, 128], F32, 512)
    ident_bf = pat("ident_bf", [128, 128], BF16, 256)
    b1f_sb = pat("b1f_sb", [128, 32], F32, 128)
    hb2f_sb = pat("hb2f_sb", [128, 8], F32, 32)
    bqk_sb = pat("bqk_sb", [128, 16], F32, 64)
    bp_sb = pat("bp_sb", [128, 8], F32, 32)
    bstage = pat("bstage", [32, 128], F32, 512)
    thr_sb = pat("thr_sb", [128, 4], F32, 32)
    scal_sb = pat("scal_sb", [1, 8], F32, 32)
    assert po[0] - base <= cap, po[0] - base

    W1fT = at("W1fT", [128, DT * FF], BF16, A_OFF)
    W2fT = at("W2fT", [128, FT * E], BF16, A_OFF + 64 * 1024)

    ao = A_OFF
    y2T = at("y2T", [128, DT * S], BF16, ao); ao += DT * S * 2
    qk_sb = [at("qk_sb0", [128, S], BF16, ao), at("qk_sb1", [128, S], BF16, ao + S * 2)]
    ao += 2 * S * 2
    kq_sb = [at("kq_sb0", [128, S], BF16, ao), at("kq_sb1", [128, S], BF16, ao + S * 2)]
    ao += 2 * S * 2
    v4a = at("v4a", [128, NSS * 4 * 65], BF16, ao); ao += NSS * 4 * 65 * 2
    pq_sb = [at(f"pq{i}", [128, 1024], BF16, ao + i * 2048) for i in range(5)]
    ao += 5 * 2048
    assert ao <= A_OFF + 128 * 1024, ao - A_OFF

    bo = B_OFF
    x0c = []
    for i in range(3):
        x0c.append(at(f"x0c{i}", [128, DT * SC], F32, bo)); bo += DT * SC * 4
    xbc = []
    for i in range(2):
        xbc.append(at(f"xbc{i}", [128, DT * SC], BF16, bo)); bo += DT * SC * 2
    xsqc = at("xsqc", [128, DT * SC], BF16, bo); bo += DT * SC * 2
    yc = []
    for i in range(2):
        yc.append(at(f"yc{i}", [128, DT * SC], BF16, bo)); bo += DT * SC * 2
    hc = at("hc", [128, FT * SC], BF16, bo); bo += FT * SC * 2
    x1oc = at("x1oc", [128, DT * SC], F32, bo); bo += DT * SC * 4
    ec = [at("ec0", [128, SC], F32, bo), at("ec1", [128, SC], F32, bo + SC * 4)]
    bo += 2 * SC * 4
    ntmp = [at(f"nt{i}", [128, SC], F32, bo + i * SC * 4) for i in range(3)]
    bo += 3 * SC * 4
    stat_m = at("stat_m", [1, SC], F32, bo); bo += SC * 4
    stat_v = at("stat_v", [1, SC], F32, bo); bo += SC * 4
    stat_sq = at("stat_sq", [1, SC], F32, bo); bo += SC * 4
    stat_m2 = at("stat_m2", [1, SC], F32, bo - SC * 4)
    stat_r = at("stat_r", [1, SC], F32, bo); bo += SC * 4
    assert bo <= P_OFF, bo - B_OFF

    po0 = B_OFF
    w1b = [at("w1b0", [128, FF], F32, po0), at("w1b1", [128, FF], F32, po0 + FF * 4)]
    po0 += 2 * FF * 4
    w2b = at("w2b", [128, FF], F32, po0); po0 += FF * 4
    w2c = at("w2c", [128, FF], F32, po0); po0 += FF * 4
    rnd = []
    for ri in range(6):
        a_ = at(f"rndA{ri}", [128, 512], BF16, po0); po0 += 512 * 2
        b_ = at(f"rndB{ri}", [128, 512], BF16, po0); po0 += 512 * 2
        rnd.append((a_, b_))
    parts = at("parts", [128, 36], F32, po0); po0 += 36 * 4
    assert po0 <= P_OFF

    po2 = B_OFF
    x1c2 = [at("x1c2_0", [128, DT * SC], F32, po2),
            at("x1c2_1", [128, DT * SC], F32, po2 + DT * SC * 4)]
    po2 += 2 * DT * SC * 4
    xbc2 = [at("xbc2_0", [128, DT * SC], BF16, po2),
            at("xbc2_1", [128, DT * SC], BF16, po2 + DT * SC * 2)]
    po2 += 2 * DT * SC * 2
    xsqc2 = at("xsqc2", [128, DT * SC], BF16, po2); po2 += DT * SC * 2

    pm = B_OFF + 28 * 1024
    mask_sb = at("mask_sb", [128, 2048], BF16, pm); pm += 4096
    bvrep_sb = at("bvrep_sb", [128, E], F32, pm); pm += E * 4
    p3o = pm
    ast_sb = [at("ast0", [64, QCW], BF16, p3o), at("ast1", [64, QCW], BF16, p3o + QCW * 2)]
    p3o += 2 * QCW * 2
    wqk_sb = [at("wqk0", [128, DT * 128], BF16, p3o),
              at("wqk1", [128, DT * 128], BF16, p3o + DT * 128 * 2)]
    p3o += 2 * DT * 128 * 2
    wv4_sb = at("wv4", [128, DT * 256], BF16, p3o); p3o += DT * 256 * 2
    v4b = at("v4b", [128, NSS * 4 * 65], BF16, p3o); p3o += NSS * 4 * 65 * 2
    ntmp2 = None
    assert p3o <= P_OFF
    pz = B_OFF
    zst = at("zst", [128, 512], F32, pz); pz += 512 * 4
    zc2 = at("zc2", [128, 32], F32, pz); pz += 32 * 4
    rz2 = at("rz2", [128, 32], BF16, pz); pz += 32 * 2 + 32
    astg = [at("astg0", [64, NQ * QCW], BF16, pz),
            at("astg1", [64, NQ * QCW], BF16, pz + NQ * QCW * 2)]
    pz += 2 * NQ * QCW * 2
    rz_sb = at("rz_sb", [NQ, 512], BF16, pz); pz += 512 * 2
    rzq = at("rzq", [1, NQ * 512], BF16, pz); pz += NQ * 512 * 2
    assert pz <= B_OFF + 28 * 1024, pz - B_OFF

    po4 = B_OFF
    a4c = [at("a4c0", [128, DT * SC], BF16, po4),
           at("a4c1", [128, DT * SC], BF16, po4 + DT * SC * 2)]
    po4 += 2 * DT * SC * 2
    x1c4 = [at("x1c4_0", [128, DT * SC], F32, po4),
            at("x1c4_1", [128, DT * SC], F32, po4 + DT * SC * 4)]
    po4 += 2 * DT * SC * 4
    x2oc = [at("x2oc0", [128, DT * SC], F32, po4),
            at("x2oc1", [128, DT * SC], F32, po4 + DT * SC * 4)]
    po4 += 2 * DT * SC * 4
    wp_sb = at("wp_sb", [128, DT * E], BF16, po4); po4 += DT * E * 2
    e4c = [at("e4c0", [128, SC], F32, po4), at("e4c1", [128, SC], F32, po4 + SC * 4)]
    po4 += 2 * SC * 4
    assert po4 <= P_OFF, po4 - B_OFF

    PPbig = nc.alloc_psum_tensor("PPbig", [128, 2048], F32)
    PPm = nc.alloc_psum_tensor("PPm", [128, 1024], F32)
    PPn = nc.alloc_psum_tensor("PPn", [128, 1024], F32)

    p = Prog(nc)
    sT = p.sem("sT"); sS = p.sem("sS"); sV = p.sem("sV"); sG = p.sem("sG")
    sLx = [p.sem("sLx0"), p.sem("sLx1"), p.sem("sLx2")]
    sLxb = [p.sem("sLxb0"), p.sem("sLxb1")]
    sLa = [p.sem("sLa0"), p.sem("sLa1")]
    sSt = [p.sem("sSt0"), p.sem("sSt1")]
    sSA = [p.sem("sSA0"), p.sem("sSA1")]
    sWst = p.sem("sWst")
    sTr = [p.sem("sTr0"), p.sem("sTr1")]
    sGw = [p.sem("sGw0"), p.sem("sGw1")]
    sZst = [p.sem("sZst0"), p.sem("sZst1")]
    sRzQ = [p.sem("sRzQ0"), p.sem("sRzQ1")]
    sW2L = [p.sem("sW2L0"), p.sem("sW2L1"), p.sem("sW2L2")]
    sWst2 = p.sem("sWst2")

    W = p.wait
    EM = p.emit

    for col, val in [(0, 1.0 / E), (1, -12.0), (2, 0.125), (3, 0.5),
                     (4, 1e-5), (5, 1.0 / (FF * E)), (6, 0.25)]:
        EM("gpsimd", lambda e, col=col, val=val: e.memset(cst[:, col:col + 1], val), inc=sG)
    EM("gpsimd", lambda e: e.memset(ones_bf[:], 1.0), inc=sG)
    EM("gpsimd", lambda e: e.memset(ones_f[:], 1.0), inc=sG)
    EM("gpsimd", lambda e: e.memset(ident_f[:], 0.0), inc=sG)
    EM("gpsimd", lambda e: e.memset(ident_bf[:], 0.0), inc=sG)
    W("gpsimd", sG)
    EM("gpsimd", lambda e: e.affine_select(
        out=ident_f[:], in_=ident_f[:], compare_op=ALU.not_equal,
        fill=1.0, base=0, pattern=[[-1, 128]], channel_multiplier=1), inc=sG)
    EM("gpsimd", lambda e: e.affine_select(
        out=ident_bf[:], in_=ident_bf[:], compare_op=ALU.not_equal,
        fill=1.0, base=0, pattern=[[-1, 128]], channel_multiplier=1), inc=sG)
    mPrepG = p.now(sG)

    def bias_transpose(src_ap, rows, dst, half):
        W("sync", sT)
        mL = p.dma("sync", sLx[0], bstage[0:rows, 0:128], src_ap)
        W("tensor", sLx[0], mL)
        W("tensor", sG, mPrepG)
        W("tensor", sS)
        EM("tensor", lambda e, rows=rows: e.transpose(
            PPm[0:128, 0:rows], bstage[0:rows, 0:128], ident_f[0:rows, 0:rows]), inc=sT)
        W("scalar", sT)
        if half:
            EM("scalar", lambda e, rows=rows, dst=dst: e.activation(
                dst[:, 0:rows], PPm[0:128, 0:rows], AF.Identity, scale=cst[:, 3:4]), inc=sS)
        else:
            EM("scalar", lambda e, rows=rows, dst=dst: e.activation(
                dst[:, 0:rows], PPm[0:128, 0:rows], AF.Identity), inc=sS)

    bias_transpose(b1f.rearrange("(t p) -> t p", p=128), 32, b1f_sb, False)
    bias_transpose(b2f.rearrange("(t p) -> t p", p=128), 8, hb2f_sb, True)
    bias_transpose(bp.rearrange("(t p) -> t p", p=128), 8, bp_sb, False)
    bias_transpose(b1a.rearrange("(h c) -> h c", c=192)[:, 0:128], 16, bqk_sb, False)

    def tern_threshold(pcol0, n_loads, thr_col, mP1, me="vector", ms=None,
                       ss_wait=None):
        if ms is None:
            ms = sV
        sb = 4 * thr_col
        tb = 2 * thr_col
        vece = me != "gpsimd"
        ev = "vector" if vece else "scalar"
        es = sV if vece else sS
        W("vector", sV, mP1)
        mTot = EM("vector", lambda e: e.reduce_sum(
            parts[:, 32 + thr_col:33 + thr_col],
            parts[:, pcol0:pcol0 + n_loads], AX.X), inc=sV)
        W("tensor", sV, mTot)
        W("tensor", sG, mPrepG)
        if ss_wait is not None:
            W("tensor", sS, ss_wait)
        mPs = EM("tensor", lambda e: e.matmul(
            PPn[0:1, thr_col:thr_col + 1], parts[:, 32 + thr_col:33 + thr_col],
            ones_f[:, 0:1], start=True, stop=True), inc=sT)
        W(ev, sT, mPs)
        if vece:
            mSc = EM(ev, lambda e: e.tensor_scalar(
                scal_sb[0:1, sb:sb + 1], PPn[0:1, thr_col:thr_col + 1],
                1.0 / (FF * E), None, ALU.mult), inc=es)
        else:
            mSc = EM(ev, lambda e: e.activation(
                scal_sb[0:1, sb:sb + 1], PPn[0:1, thr_col:thr_col + 1],
                AF.Identity, scale=cst[0:1, 5:6]), inc=es)
        W(me, es, mSc)
        EM(me, lambda e: e.tensor_scalar(
            scal_sb[0:1, sb + 1:sb + 2], scal_sb[0:1, sb:sb + 1], 1e-8, None,
            ALU.max), inc=ms)
        W(me, ms)
        EM(me, lambda e: e.tensor_scalar(
            scal_sb[0:1, sb + 2:sb + 3], scal_sb[0:1, sb + 1:sb + 2], 0.5, None,
            ALU.mult), inc=ms)
        W(me, ms)
        EM(me, lambda e: e.tensor_scalar(
            scal_sb[0:1, sb + 3:sb + 4], scal_sb[0:1, sb + 1:sb + 2], -0.5, None,
            ALU.mult), inc=ms)
        mThr = p.now(ms)
        W("tensor", ms, mThr)
        mBc = EM("tensor", lambda e: e.matmul(
            PPn[0:128, 512 + tb:514 + tb], ones_f[0:1, :],
            scal_sb[0:1, sb + 2:sb + 4], start=True, stop=True), inc=sT)
        W(ev, sT, mBc)
        if vece:
            mThrS = EM(ev, lambda e: e.tensor_copy(
                thr_sb[:, tb:tb + 2], PPn[0:128, 512 + tb:514 + tb]), inc=es)
        else:
            mThrS = EM(ev, lambda e: e.activation(
                thr_sb[:, tb:tb + 2], PPn[0:128, 512 + tb:514 + tb],
                AF.Identity), inc=es)
        return es, mThrS

    def w1f_src(ld):
        return W1f.rearrange("(b p) d -> p b d", p=128)[:, ld * 4:(ld + 1) * 4, :]

    def w1b_dst(buf):
        return buf[:].rearrange("p (a d) -> p a d", a=4)

    mRed1, mRed2, mLW2, mLd1 = {}, {}, {}, {}
    for ld in range(8):
        if ld >= 2:
            W("sync", sV, mRed1[ld - 2])
        mL = p.dma("sync", sLx[ld % 2], w1b_dst(w1b[ld % 2]), w1f_src(ld))
        mLd1[ld] = mL
        W("vector", sLx[ld % 2], mL)
        mRed1[ld] = EM("vector", lambda e, ld=ld: e.reduce_sum(
            parts[:, ld:ld + 1], w1b[ld % 2][:], AX.X,
            apply_absolute_value=True), inc=sV)
        if ld >= 2:
            W("scalar", sV, mRed2[ld - 2])
        w2buf = w2b if ld % 2 == 0 else w2c
        mLW2[ld] = p.dma("scalar", sW2L[0], w2buf[:],
                         W2f[ld * 128:(ld + 1) * 128, :])
        W("vector", sW2L[0], mLW2[ld])
        mRed2[ld] = EM("vector", lambda e, w2buf=w2buf, ld=ld: e.reduce_sum(
            parts[:, 16 + ld:17 + ld], w2buf[:], AX.X,
            apply_absolute_value=True), inc=sV)
    mP1a = p.now(sV)
    mThrS1 = tern_threshold(0, 8, 0, mP1a, me="gpsimd", ms=sG,
                            ss_wait=p.now(sS))[1]
    mP1b = p.now(sV)
    mThrS2 = tern_threshold(16, 8, 1, mP1b, me="gpsimd", ms=sG,
                            ss_wait=mThrS1)[1]

    bank1 = [PPbig[:, 0:512], PPbig[:, 512:1024], PPm[:, 0:512]]
    bank2 = [PPbig[:, 1024:1536], PPbig[:, 1536:2048], PPn[:, 0:512]]
    order1 = [6, 7, 0, 1, 2, 3, 4, 5]
    mTrG, mQG, mTrC = {}, {}, {}
    mTr2G, mQ2G, mTr2C = {}, {}, {}
    gi = 0
    gj = 0
    for i in range(8):
        ld = order1[i]
        buf = w1b[i % 2]
        if i >= 2:
            W("sync", sT, mTrC[order1[i - 2]])
            mLh = p.dma("sync", sLxb[i % 2],
                        w1b_dst(buf)[:, :, 0:512], w1f_src(ld)[:, :, 0:512])
            mL = p.dma("sync", sLx[i % 2],
                       w1b_dst(buf)[:, :, 512:1024], w1f_src(ld)[:, :, 512:1024])
            hsem = sLxb[i % 2]
        else:
            mLh = mL = mLd1[ld]
            hsem = sLx[i % 2]
        for t in range(DT):
            if t < 4:
                W("tensor", hsem, mLh)
            else:
                W("tensor", sLx[i % 2], mL)
            W("tensor", sG, mPrepG)
            W("tensor", sS, mThrS2)
            if gi >= 3:
                W("tensor", sG, mQG[gi - 3])
            pb1 = bank1[gi % 3]
            for a in range(4):
                EM("tensor", lambda e, t=t, a=a, pb1=pb1, buf=buf: e.transpose(
                    pb1[:, a * 128:(a + 1) * 128],
                    buf[:, a * E + t * 128: a * E + t * 128 + 128],
                    ident_f[:, :]), inc=(sT if a == 3 else None))
            mTrG[(ld, t)] = p.now(sT)
            rp = rnd[gi % 3]
            W("scalar", sT, mTrG[(ld, t)])
            W("scalar", sS, mThrS1)
            if gi >= 3:
                W("scalar", sG, mQG[gi - 3])
            EM("scalar", lambda e, pb1=pb1, rp=rp: e.activation(
                rp[0][:], pb1[:, 0:512], AF.Sign,
                bias=thr_sb[:, 1:2]), inc=sS)
            mSb = EM("scalar", lambda e, pb1=pb1, rp=rp: e.activation(
                rp[1][:], pb1[:, 0:512], AF.Sign,
                bias=thr_sb[:, 0:1]), inc=sS)
            W("gpsimd", sS, mSb)
            mQG[gi] = EM("gpsimd", lambda e, ld=ld, t=t, rp=rp: e.tensor_add(
                W1fT[:, t * FF + ld * 512: t * FF + ld * 512 + 512],
                rp[0][:], rp[1][:]), inc=sG)
            gi += 1
        mTrC[ld] = p.now(sT)
        ld2 = order1[i]
        buf2 = w2b if i % 2 == 0 else w2c
        if i >= 2:
            W("gpsimd", sT, mTr2C[order1[i - 2]])
            mL2h = p.dma("gpsimd", sW2L[2], buf2[:, 0:2048],
                         W2f[ld2 * 128:(ld2 + 1) * 128, 0:2048])
            mL2 = p.dma("gpsimd", sW2L[1], buf2[:, 2048:4096],
                        W2f[ld2 * 128:(ld2 + 1) * 128, 2048:4096])
            l2sem, l2hsem = sW2L[1], sW2L[2]
        else:
            mL2h = mL2 = mLW2[ld2]
            l2sem = l2hsem = sW2L[0]
        for g4 in range(FT // 4):
            if g4 < 4:
                W("tensor", l2hsem, mL2h)
            else:
                W("tensor", l2sem, mL2)
            W("tensor", sG, mPrepG)
            W("tensor", sS, mThrS2)
            if gj >= 3:
                W("tensor", sV, mQ2G[gj - 3])
            pb2 = bank2[gj % 3]
            for ff in range(4):
                f = g4 * 4 + ff
                EM("tensor", lambda e, f=f, ff=ff, pb2=pb2, buf2=buf2:
                   e.transpose(
                    pb2[:, ff * 128:(ff + 1) * 128],
                    buf2[:, f * 128:(f + 1) * 128],
                    ident_f[:, :]), inc=(sT if ff == 3 else None))
            mTr2G[gj] = p.now(sT)
            rp = rnd[3 + gj % 3]
            W("scalar", sT, mTr2G[gj])
            W("scalar", sS, mThrS2)
            if gj >= 3:
                W("scalar", sV, mQ2G[gj - 3])
            EM("scalar", lambda e, pb2=pb2, rp=rp: e.activation(
                rp[0][:], pb2[:, 0:512], AF.Sign,
                bias=thr_sb[:, 3:4]), inc=sS)
            mSb2 = EM("scalar", lambda e, pb2=pb2, rp=rp: e.activation(
                rp[1][:], pb2[:, 0:512], AF.Sign,
                bias=thr_sb[:, 2:3]), inc=sS)
            W("vector", sS, mSb2)
            for ff in range(4):
                f = g4 * 4 + ff
                EM("vector", lambda e, f=f, ld2=ld2, ff=ff, rp=rp:
                   e.tensor_add(
                    W2fT[:, f * E + ld2 * 128: f * E + ld2 * 128 + 128],
                    rp[0][:, ff * 128:(ff + 1) * 128],
                    rp[1][:, ff * 128:(ff + 1) * 128]), inc=(
                        sV if ff == 3 else None))
            mQ2G[gj] = p.now(sV)
            gj += 1
        mTr2C[ld2] = p.now(sT)
    mW1Done = p.now(sG)
    mW2Done = p.now(sV)
    mW1TrAll = p.now(sT)
    mW2TrAll = p.now(sT)

    W("gpsimd", sG, mW1Done)
    HW = DT * FF // 2
    p.dma("gpsimd", sWst, w1t_hbm[:, 0:HW], W1fT[:, 0:HW])
    mW1img = p.dma("gpsimd", sWst, w1t_hbm[:, HW:2 * HW], W1fT[:, HW:2 * HW])
    W("gpsimd", sV, mW2Done)
    p.dma("gpsimd", sWst2, w2t_hbm[:, 0:HW], W2fT[:, 0:HW])
    mW2img = p.dma("gpsimd", sWst2, w2t_hbm[:, HW:2 * HW], W2fT[:, HW:2 * HW])

    W("sync", sT, mW2TrAll)
    for eng_ in ("scalar", "vector"):
        W(eng_, sT, mW2TrAll)
        W(eng_, sV, mW2Done)
        W(eng_, sG, mW1Done)
        W(eng_, sS, p.now(sS))
    W("tensor", sG, mW1Done)
    W("tensor", sS, mThrS1)
    W("tensor", sS, mThrS2)

    def ffn_phase(x_in_dram, x_out_dram, pre_mm2=None):
        mX, mXb, mStats, mMV, mRstd, mBcast, mNorm = {}, {}, {}, {}, {}, {}, {}
        mGelu, mMM1, mMM2, mE, mAdd, mEpi, mStore = {}, {}, {}, {}, {}, {}, {}

        def load(c):
            if c >= NCH:
                return
            if c >= 3:
                W("sync", sV, mEpi[c - 3])
            mX[c] = p.dma("sync", sLx[c % 3],
                          x0c[c % 3][:].rearrange("p (t s) -> p t s", t=DT),
                          xchunk_ap(x_in_dram, c * SC, SC))

        def ln_front(c):
            if c >= NCH:
                return
            W("scalar", sLx[c % 3], mX[c])
            if c >= 2:
                W("scalar", sT, mStats[c - 2])
                W("scalar", sV, mNorm[c - 2])
            EM("scalar", lambda e, c=c: e.copy(xbc[c % 2][:], x0c[c % 3][:]), inc=sS)
            if c >= 1:
                W("scalar", sT, mStats[c - 1])
            W("scalar", sS)
            mXb[c] = EM("scalar", lambda e, c=c: e.square(xsqc[:], xbc[c % 2][:]), inc=sS)
            W("tensor", sS, mXb[c])
            if c >= 1:
                W("tensor", sS, mMV[c - 1])
            for t in range(DT):
                EM("tensor", lambda e, c=c, t=t: e.matmul(
                    PPm[0:1, 0:SC], ones_bf[:, 0:1],
                    xbc[c % 2][:, t * SC:(t + 1) * SC],
                    start=(t == 0), stop=(t == DT - 1)),
                    inc=(sT if t == DT - 1 else None))
            for t in range(DT):
                EM("tensor", lambda e, t=t: e.matmul(
                    PPm[0:1, 512:512 + SC], ones_bf[:, 0:1],
                    xsqc[:, t * SC:(t + 1) * SC],
                    start=(t == 0), stop=(t == DT - 1)),
                    inc=(sT if t == DT - 1 else None))
            mStats[c] = p.now(sT)

        def ln_mid(c):
            if c >= NCH:
                return
            W("scalar", sT, mStats[c])
            if c >= 1:
                W("scalar", sT, mBcast[c - 1])
            EM("scalar", lambda e: e.activation(
                stat_m[0:1, 0:SC], PPm[0:1, 0:SC], AF.Identity,
                scale=cst[0:1, 0:1]), inc=sS)
            W("scalar", sS)
            EM("scalar", lambda e: e.square(stat_m2[0:1, 0:SC], stat_m[0:1, 0:SC]),
               inc=sS)
            EM("scalar", lambda e: e.activation(
                stat_v[0:1, 0:SC], PPm[0:1, 512:512 + SC], AF.Identity,
                scale=cst[0:1, 0:1]), inc=sS)
            mMV[c] = p.now(sS)
            W("vector", sS, mMV[c])
            if c >= 1:
                W("vector", sT, mBcast[c - 1])
            EM("vector", lambda e: e.tensor_sub(
                stat_v[0:1, 0:SC], stat_v[0:1, 0:SC], stat_m2[0:1, 0:SC]), inc=sV)
            W("scalar", sV)
            EM("scalar", lambda e: e.activation(
                stat_sq[0:1, 0:SC], stat_v[0:1, 0:SC], AF.Sqrt,
                bias=cst[0:1, 4:5]), inc=sS)
            W("vector", sS)
            mRstd[c] = EM("vector", lambda e: e.reciprocal(
                stat_r[0:1, 0:SC], stat_sq[0:1, 0:SC]), inc=sV)

        def ln_bcast(c):
            if c >= NCH:
                return
            W("tensor", sV, mRstd[c])
            if c >= 1:
                W("tensor", sV, mNorm[c - 1])
            EM("tensor", lambda e: e.matmul(
                PPn[:, 0:SC], ones_f[0:1, :], stat_m[0:1, 0:SC],
                start=True, stop=True), inc=sT)
            EM("tensor", lambda e: e.matmul(
                PPn[:, 512:512 + SC], ones_f[0:1, :], stat_r[0:1, 0:SC],
                start=True, stop=True), inc=sT)
            mBcast[c] = p.now(sT)

        def ln_norm(c):
            if c >= NCH:
                return
            W("vector", sT, mBcast[c])
            if c >= 2:
                W("vector", sS, mMM1[c - 2])
            mSubT, mMulT = {}, {}
            t = 0
            for gsz in (3, 3, 2):
                for i in range(gsz):
                    tt_ = t + i
                    if tt_ >= 3:
                        W("vector", sV, mMulT[tt_ - 3])
                    mSubT[tt_] = EM("vector", lambda e, c=c, tt_=tt_, i2=tt_ % 3:
                                    e.tensor_sub(
                        ntmp[i2][:], xbc[c % 2][:, tt_ * SC:(tt_ + 1) * SC],
                        PPn[:, 0:SC]), inc=sV)
                for i in range(gsz):
                    tt_ = t + i
                    W("vector", sV, mSubT[tt_])
                    mMulT[tt_] = EM("vector", lambda e, c=c, tt_=tt_, i2=tt_ % 3:
                                    e.tensor_mul(
                        yc[c % 2][:, tt_ * SC:(tt_ + 1) * SC], ntmp[i2][:],
                        PPn[:, 512:512 + SC]), inc=sV)
                t += gsz
            mNorm[c] = p.now(sV)

        def mm1(c):
            W("tensor", sV, mNorm[c])
            for f in range(FT):
                pb = PPbig[:, (f % 2) * 512:(f % 2) * 512 + SC]
                if f >= 2:
                    W("tensor", sS, mGelu[(c, f - 2)])
                elif c >= 1:
                    W("tensor", sS, mGelu[(c - 1, FT - 2 + f)])
                for t in range(DT):
                    EM("tensor", lambda e, c=c, f=f, t=t, pb=pb: e.matmul(
                        pb, W1fT[:, t * FF + f * 128: t * FF + f * 128 + 128],
                        yc[c % 2][:, t * SC:(t + 1) * SC],
                        start=(t == 0), stop=(t == DT - 1)),
                        inc=(sT if t == DT - 1 else None))
                W("scalar", sT)
                if c >= 1 and f == 0:
                    W("scalar", sT, mMM2[c - 1])
                mGelu[(c, f)] = EM("scalar", lambda e, f=f, pb=pb: e.activation(
                    hc[:, f * SC:(f + 1) * SC], pb, AF.Gelu,
                    bias=b1f_sb[:, f:f + 1], scale=cst[:, 3:4]), inc=sS)
            mMM1[c] = p.now(sS)

        def mm2_epi(c):
            if c == 0 and pre_mm2 is not None:
                pre_mm2()
            for dch in range(DT):
                pb = PPbig[:, 1024 + (dch % 2) * 512:1024 + (dch % 2) * 512 + SC]
                if dch >= 2:
                    W("tensor", sS, mE[(c, dch - 2)])
                elif c >= 1:
                    W("tensor", sS, mE[(c - 1, DT - 2 + dch)])
                for f in range(FT):
                    W("tensor", sS, mGelu[(c, f)])
                    EM("tensor", lambda e, c=c, f=f, dch=dch, pb=pb: e.matmul(
                        pb, W2fT[:, f * E + dch * 128: f * E + dch * 128 + 128],
                        hc[:, f * SC:(f + 1) * SC],
                        start=(f == 0), stop=(f == FT - 1)),
                        inc=(sT if f == FT - 1 else None))
                W("scalar", sT)
                if dch >= 2:
                    W("scalar", sV, mAdd[(c, dch - 2)])
                elif c >= 1:
                    W("scalar", sV, mAdd[(c - 1, DT - 2 + dch)])
                mE[(c, dch)] = EM("scalar", lambda e, dch=dch, pb=pb: e.activation(
                    ec[dch % 2][:], pb, AF.Identity,
                    bias=hb2f_sb[:, dch:dch + 1], scale=cst[:, 6:7]), inc=sS)
                W("vector", sS, mE[(c, dch)])
                if dch == 0 and c >= 1:
                    W("vector", sSt[(c - 1) % 2], mStore[c - 1])
                mAdd[(c, dch)] = EM("vector", lambda e, c=c, dch=dch: e.tensor_add(
                    x1oc[:, dch * SC:(dch + 1) * SC], ec[dch % 2][:],
                    x0c[c % 3][:, dch * SC:(dch + 1) * SC]), inc=sV)
            mMM2[c] = p.now(sT)
            mEpi[c] = p.now(sV)
            W("gpsimd", sV, mEpi[c])
            mStore[c] = p.dma("gpsimd", sSt[c % 2],
                              xchunk_ap(x_out_dram, c * SC, SC),
                              x1oc[:].rearrange("p (t s) -> p t s", t=DT))

        load(0); load(1); load(2)
        ln_front(0); ln_mid(0); ln_bcast(0); ln_norm(0)
        ln_front(1); ln_mid(1)
        for c in range(NCH):
            mm1(c)
            ln_bcast(c + 1); ln_norm(c + 1)
            ln_front(c + 2); ln_mid(c + 2)
            mm2_epi(c)
            load(c + 3)

    def _w2ft_ready():
        W("tensor", sV, mW2Done)
    ffn_phase(xT_in, x1_hbm, pre_mm2=_w2ft_ready)
    p.barrier()

    EM("gpsimd", lambda e: e.memset(mask_sb[:], 0.0), inc=sG)
    W("gpsimd", sG)
    for o_i, o in enumerate((0, 128, 256, 384)):
        EM("gpsimd", lambda e, o_i=o_i, o=o: e.affine_select(
            out=mask_sb[:, o_i * 512:(o_i + 1) * 512],
            in_=mask_sb[:, o_i * 512:(o_i + 1) * 512],
            compare_op=ALU.is_ge, fill=1.0, base=o - 1,
            pattern=[[-1, 512]], channel_multiplier=1), inc=sG)
    mMasks = p.now(sG)
    mBv = p.dma("sync", sLa[0],
                bvrep_sb[0:1, :].rearrange("p (h c) -> p h c", h=H),
                b1a.rearrange("(h c) -> h c", c=192)[:, 128:192])
    W("tensor", sLa[0], mBv)
    EM("tensor", lambda e: e.matmul(
        PPm[:, 0:512], ones_f[0:1, :], bvrep_sb[0:1, 0:512],
        start=True, stop=True), inc=sT)
    EM("tensor", lambda e: e.matmul(
        PPm[:, 512:1024], ones_f[0:1, :], bvrep_sb[0:1, 512:1024],
        start=True, stop=True), inc=sT)
    W("scalar", sT)
    mBvRep = EM("scalar", lambda e: e.activation(
        bvrep_sb[:, :], PPm[:, 0:1024], AF.Identity), inc=sS)

    mX2, mXb2, mSt2, mMV2, mRs2, mBc2, mN2 = {}, {}, {}, {}, {}, {}, {}

    def ln2_load(c):
        if c >= NCH:
            return
        if c >= 2:
            W("sync", sS, mXb2[c - 2])
        mX2[c] = p.dma("sync", sLx[c % 2],
                       x1c2[c % 2][:].rearrange("p (t s) -> p t s", t=DT),
                       xchunk_ap(x1_hbm, c * SC, SC))

    def ln2_front(c):
        if c >= NCH:
            return
        W("scalar", sLx[c % 2], mX2[c])
        if c >= 2:
            W("scalar", sT, mSt2[c - 2])
            W("scalar", sV, mN2[c - 2])
        EM("scalar", lambda e, c=c: e.copy(xbc2[c % 2][:], x1c2[c % 2][:]), inc=sS)
        if c >= 1:
            W("scalar", sT, mSt2[c - 1])
        W("scalar", sS)
        mXb2[c] = EM("scalar", lambda e, c=c: e.square(xsqc2[:], xbc2[c % 2][:]), inc=sS)
        W("tensor", sS, mXb2[c])
        stp = PPm if c % 2 == 0 else PPn
        if c >= 2:
            W("tensor", sS, mMV2[c - 2])
        for t in range(DT):
            EM("tensor", lambda e, c=c, t=t, stp=stp: e.matmul(
                stp[0:1, 0:SC], ones_bf[:, 0:1],
                xbc2[c % 2][:, t * SC:(t + 1) * SC],
                start=(t == 0), stop=(t == DT - 1)), inc=(sT if t == DT - 1 else None))
        for t in range(DT):
            EM("tensor", lambda e, t=t, stp=stp: e.matmul(
                stp[0:1, 512:512 + SC], ones_bf[:, 0:1],
                xsqc2[:, t * SC:(t + 1) * SC],
                start=(t == 0), stop=(t == DT - 1)), inc=(sT if t == DT - 1 else None))
        mSt2[c] = p.now(sT)

    def ln2_mid(c):
        if c >= NCH:
            return
        W("scalar", sT, mSt2[c])
        stp = PPm if c % 2 == 0 else PPn
        if c >= 1:
            W("scalar", sT, mBc2[c - 1])
        EM("scalar", lambda e, stp=stp: e.activation(
            stat_m[0:1, 0:SC], stp[0:1, 0:SC], AF.Identity, scale=cst[0:1, 0:1]), inc=sS)
        W("scalar", sS)
        EM("scalar", lambda e: e.square(stat_m2[0:1, 0:SC], stat_m[0:1, 0:SC]), inc=sS)
        EM("scalar", lambda e, stp=stp: e.activation(
            stat_v[0:1, 0:SC], stp[0:1, 512:512 + SC], AF.Identity,
            scale=cst[0:1, 0:1]), inc=sS)
        mMV2[c] = p.now(sS)
        W("vector", sS, mMV2[c])
        if c >= 1:
            W("vector", sT, mBc2[c - 1])
        EM("vector", lambda e: e.tensor_sub(
            stat_v[0:1, 0:SC], stat_v[0:1, 0:SC], stat_m2[0:1, 0:SC]), inc=sV)
        W("scalar", sV)
        EM("scalar", lambda e: e.activation(
            stat_sq[0:1, 0:SC], stat_v[0:1, 0:SC], AF.Sqrt, bias=cst[0:1, 4:5]), inc=sS)
        W("vector", sS)
        mRs2[c] = EM("vector", lambda e: e.reciprocal(
            stat_r[0:1, 0:SC], stat_sq[0:1, 0:SC]), inc=sV)

    def ln2_back(c):
        if c >= NCH:
            return
        W("tensor", sV, mRs2[c])
        bq = 0 if c % 2 == 0 else 1024
        if c >= 2:
            W("tensor", sV, mN2[c - 2])
        EM("tensor", lambda e, bq=bq: e.matmul(
            PPbig[:, bq:bq + SC], ones_f[0:1, :], stat_m[0:1, 0:SC],
            start=True, stop=True), inc=sT)
        EM("tensor", lambda e, bq=bq: e.matmul(
            PPbig[:, bq + 512:bq + 512 + SC], ones_f[0:1, :], stat_r[0:1, 0:SC],
            start=True, stop=True), inc=sT)
        mBc2[c] = p.now(sT)
        W("vector", sT, mBc2[c])
        mSubT, mMulT = {}, {}
        t = 0
        for gsz in (3, 3, 2):
            for i in range(gsz):
                tt_ = t + i
                if tt_ >= 3:
                    W("vector", sV, mMulT[tt_ - 3])
                mSubT[tt_] = EM("vector", lambda e, c=c, tt_=tt_, i2=tt_ % 3, bq=bq:
                                e.tensor_sub(
                    ntmp[i2][:], xbc2[c % 2][:, tt_ * SC:(tt_ + 1) * SC],
                    PPbig[:, bq:bq + SC]), inc=sV)
            for i in range(gsz):
                tt_ = t + i
                W("vector", sV, mSubT[tt_])
                mMulT[tt_] = EM("vector", lambda e, c=c, tt_=tt_, i2=tt_ % 3, bq=bq:
                                e.tensor_mul(
                    y2T[:, tt_ * S + c * SC: tt_ * S + c * SC + SC],
                    ntmp[i2][:], PPbig[:, bq + 512:bq + 512 + SC]), inc=sV)
            t += gsz
        mN2[c] = p.now(sV)

    ln2_load(0); ln2_load(1)
    ln2_front(0); ln2_mid(0)
    for c in range(NCH):
        ln2_load(c + 2)
        ln2_back(c)
        ln2_front(c + 1); ln2_mid(c + 1)
    p.barrier()

    NG = H // 4
    v4_sb = [v4a, v4b]
    mWqk, mWv, mVG, mQKdone, mDup = {}, {}, {}, {}, {}
    mScAll, mAvDone, mAstStore = {}, {}, {}
    mEvacPair, mZstore, mRecip, mRzQ = {}, {}, {}, {}
    mNormBc, mNormMult = {}, {}
    lastAvHalf = {}
    lastExpOn = {}
    pqLastAv = {}
    lastFillEvac = [None, None]
    lastNormBc = [None, None]
    lastNormMult = [None, None]
    lastZst = [None, None]
    fillq = []
    normq = []

    for vb in range(2):
        EM("gpsimd", lambda e, vb=vb: e.memset(
            v4_sb[vb][:].rearrange("p (ss j c) -> p ss j c", j=4, c=65)[:, :, :, 64:65],
            1.0), inc=sG)
    mOnesV = p.now(sG)

    def wqk_load(h):
        if h >= H:
            return
        if h >= 2 and (h - 2) in mQKdone:
            W("gpsimd", sT, mQKdone[h - 2][1])
        src_ = W1a.rearrange("(t p) f -> p t f", p=128)[:, :, h * 192: h * 192 + 128]
        mWqk[h] = p.dma("gpsimd", sGw[h % 2],
                        wqk_sb[h % 2][:].rearrange("p (t f) -> p t f", t=DT), src_)

    def wv_load(g):
        if g >= NG:
            return
        if g >= 1:
            W("gpsimd", sT, mVG[g - 1][1])
        for j in range(4):
            hh = g * 4 + j
            src_ = W1a.rearrange("(t p) f -> p t f", p=128)[
                :, :, hh * 192 + 128: hh * 192 + 192]
            dst = wv4_sb[:].rearrange("p (t f) -> p t f", t=DT)[:, :, j * 64:(j + 1) * 64]
            mWv[g] = p.dma("gpsimd", sGw[g % 2], dst, src_)

    def wait_fill_bank(eng):
        kind, m = lastFillEvac
        if m is not None:
            W(eng, sS if kind == "S" else sV, m)

    def fill_v(g, ss):
        W("tensor", sGw[g % 2], mWv[g])
        W("tensor", sG, mOnesV)
        if g >= 2:
            W("tensor", sT, mScAll.get((g - 2) * 4 + 3, 0) or 0)
        wait_fill_bank("tensor")
        for t in range(DT):
            EM("tensor", lambda e, ss=ss, t=t: e.matmul(
                PPm[:, 512:768],
                y2T[:, t * S + ss * 128: t * S + ss * 128 + 128],
                wv4_sb[:, t * 256:(t + 1) * 256],
                start=(t == 0), stop=(t == DT - 1)),
                inc=(sT if t == DT - 1 else None))
        mV = p.now(sT)
        W("vector", sT, mV)
        W("vector", sS, mBvRep)
        mE_ = EM("vector", lambda e, g=g, ss=ss: e.tensor_add(
            v4_sb[g % 2][:].rearrange("p (s2 j c) -> p s2 j c", j=4, c=65)[
                :, ss, :, 0:64],
            PPm[:, 512:768], bvrep_sb[:, g * 256:(g + 1) * 256]), inc=sV)
        lastFillEvac[0], lastFillEvac[1] = "V", mE_
        if ss == NSS - 1:
            mVG[g] = (p.now(sV), p.now(sT))

    def fill_qk(h, qs):
        W("tensor", sGw[h % 2], mWqk[h])
        if h >= 2 and (h - 2) in mScAll:
            W("tensor", sT, mScAll[h - 2])
        wait_fill_bank("tensor")
        for t in range(DT):
            EM("tensor", lambda e, h=h, qs=qs, t=t: e.matmul(
                PPm[0:128, 512:1024],
                wqk_sb[h % 2][:, t * 128: t * 128 + 128],
                y2T[:, t * S + qs * 512: t * S + qs * 512 + 512],
                start=(t == 0), stop=(t == DT - 1)),
                inc=(sT if t == DT - 1 else None))
        W("vector", sT)
        mC = EM("vector", lambda e, h=h, qs=qs: e.tensor_scalar(
            qk_sb[h % 2][0:128, qs * 512:(qs + 1) * 512], PPm[0:128, 512:1024],
            bqk_sb[:, h:h + 1], None, ALU.add), inc=sV)
        lastFillEvac[0], lastFillEvac[1] = "V", mC
        halves = []
        if qs == max(NQ // 2 - 1, 0):
            halves.append(0)
        if qs == NQ - 1:
            halves.append(1)
        for half in halves:
            c0, c1 = half * (S // 2), (half + 1) * (S // 2)
            sdp = sLa[half]
            W("sync", sV, mC)
            if h >= 2 and (h - 2) in mScAll:
                W("sync", sT, mScAll[h - 2])
            p.dma("sync", sdp, kq_sb[h % 2][64:128, c0:c1],
                  qk_sb[h % 2][0:64, c0:c1])
            mDup[(h, half)] = p.dma("sync", sdp, kq_sb[h % 2][0:64, c0:c1],
                                    qk_sb[h % 2][64:128, c0:c1])
        if qs == NQ - 1:
            mQKdone[h] = (p.now(sV), p.now(sT))

    def enqueue_head(h, with_loads=False):
        if h >= H:
            return
        if with_loads:
            wqk_load(h)
            if h % 4 == 0:
                wv_load(h // 4)
        if h % 4 == 0:
            for ss in range(NSS):
                fillq.append(lambda g=h // 4, ss=ss: fill_v(g, ss))
        for qs in range(NQ):
            fillq.append(lambda h=h, qs=qs: fill_qk(h, qs))

    def pop_fill():
        if fillq:
            fillq.pop(0)()

    def drain_fill():
        while fillq:
            fillq.pop(0)()

    def norm_job(h, qc):
        kk_ = h * NQ + qc
        slot = kk_ % 2
        r0 = slot * 64
        W("tensor", sRzQ[0], mRzQ[h])
        if lastNormMult[slot] is not None:
            W("tensor", sV, lastNormMult[slot])
        mNormBc[(h, qc)] = EM("tensor", lambda e, qc=qc, r0=r0: e.matmul(
            PPm[r0:r0 + 64, 0:512], ones_bf[0:1, 0:64],
            rzq[0:1, qc * 512:(qc + 1) * 512],
            start=True, stop=True), inc=sT)
        lastNormBc[0] = mNormBc[(h, qc)]
        W("vector", sT, mNormBc[(h, qc)])
        if kk_ >= 2:
            W("vector", sSA[kk_ % 2], mAstStore[kk_ - 2])
        mNormMult[(h, qc)] = EM("vector", lambda e, h=h, qc=qc, kk_=kk_, r0=r0:
                                e.tensor_mul(
            ast_sb[kk_ % 2][0:64, 0:512],
            astg[h % 2][:, qc * 512:(qc + 1) * 512],
            PPm[r0:r0 + 64, 0:512]), inc=sV)
        lastNormMult[slot] = mNormMult[(h, qc)]
        W("sync", sV, mNormMult[(h, qc)])
        mAstStore[kk_] = p.dma("sync", sSA[kk_ % 2],
                               a_hbm[h * 64:(h + 1) * 64,
                                     qc * 512:(qc + 1) * 512],
                               ast_sb[kk_ % 2][0:64, 0:512])

    def pop_norm(n):
        for _ in range(n):
            if normq:
                normq.pop(0)()

    def attn_head(h):
        g = h // 4
        j = h % 4
        qksb, kqsb = qk_sb[h % 2], kq_sb[h % 2]
        pairs = []
        for qc in range(NQ):
            npair = 2 * (qc + 1)
            for pr in range(npair):
                pairs.append((qc, pr, npair))
        NPH = len(pairs)
        mExpP, mMskP, prPq = {}, {}, {}

        def epilogue(qc):
            avsl = (qc % 2) * 512
            W("vector", sT, mAvDone[(h, qc)])
            EM("vector", lambda e, h=h, qc=qc, avsl=avsl: e.tensor_copy(
                astg[h % 2][:, qc * 512:(qc + 1) * 512],
                PPn[0:64, avsl:avsl + 512]), inc=sV)
            if lastZst[0] is not None:
                W("vector", sZst[0], lastZst[0])
            EM("vector", lambda e, avsl=avsl: e.tensor_copy(
                zst[64:65, 0:512], PPn[64:65, avsl:avsl + 512]), inc=sV)
            mEvacPair[(h, qc)] = p.now(sV)
            W("gpsimd", sV, mEvacPair[(h, qc)])
            if qc == 0 and h >= 1:
                W("gpsimd", sV, mRecip[h - 1])
            mZstore[(h, qc)] = p.dma(
                "gpsimd", sZst[0], zc2[qc * 16:(qc + 1) * 16, :],
                zst[64:65, 0:512].rearrange("p (a s) -> p a s", a=16))
            lastZst[0] = mZstore[(h, qc)]

        def stage_a(i):
            qc, pr, npair = pairs[i]
            kc0 = 2 * pr
            sc0 = (i % 2) * 1024
            le = lastExpOn.get(i % 2)
            if le is not None:
                W("tensor", sS, le)
            W("tensor", sV, mQKdone[h][0])
            W("tensor", sT, mQKdone[h][1])
            W("tensor", sLa[0], mDup[(h, 0)])
            if (qc + 1) * 512 > S // 2 or (kc0 + 2) * 128 > S // 2:
                W("tensor", sLa[1], mDup[(h, 1)])
            W("tensor", sV, mVG[g][0])
            W("tensor", sT, mVG[g][1])
            for rt in range(2):
                kc = kc0 + rt
                r0 = rt * 64
                ktile = kqsb if rt == 0 else qksb
                qtile = qksb if rt == 0 else kqsb
                EM("tensor", lambda e, qc=qc, kc=kc, rt=rt, r0=r0, sc0=sc0,
                   ktile=ktile, qtile=qtile: e.matmul(
                    PPbig[0:128, sc0 + rt * 512: sc0 + rt * 512 + 512],
                    ktile[r0:r0 + 64, kc * 128:(kc + 1) * 128],
                    qtile[r0:r0 + 64, qc * 512:(qc + 1) * 512],
                    start=True, stop=True), inc=(sT if rt == 1 else None))
            mScP = p.now(sT)
            pq_i = i % 5
            prPq[i] = pq_i
            W("scalar", sT, mScP)
            la = pqLastAv.get(pq_i)
            if la is not None:
                W("scalar", sT, la)
            mExpP[i] = EM("scalar", lambda e, pq_i=pq_i, sc0=sc0: e.activation(
                pq_sb[pq_i][:, 0:1024], PPbig[:, sc0:sc0 + 1024], AF.Exp,
                bias=cst[:, 1:2], scale=cst[:, 2:3]), inc=sS)
            lastExpOn[i % 2] = mExpP[i]
            dpr = pr - (npair - 2)
            if dpr >= 0:
                W("vector", sS, mExpP[i])
                mMskP[i] = EM("vector", lambda e, pq_i=pq_i, dpr=dpr: e.tensor_mul(
                    pq_sb[pq_i][:, 0:1024], pq_sb[pq_i][:, 0:1024],
                    mask_sb[:, dpr * 1024:(dpr + 1) * 1024]), inc=sV)

        def stage_b(i):
            qc, pr, npair = pairs[i]
            kc0 = 2 * pr
            nk = 2 * npair
            avsl = (qc % 2) * 512
            pq_i = prPq[i]
            if pr == 0:
                prev = lastAvHalf.get(qc % 2)
                if prev is not None:
                    W("tensor", sV, mEvacPair[prev])
                lastAvHalf[qc % 2] = (h, qc)
            if i in mMskP:
                W("tensor", sV, mMskP[i])
            else:
                W("tensor", sS, mExpP[i])
            for kk in range(2):
                kc = kc0 + kk
                EM("tensor", lambda e, kc=kc, kk=kk, pq_i=pq_i, j=j, avsl=avsl,
                   st=(kc0 + kk == 0), sp=(kc0 + kk == nk - 1): e.matmul(
                    PPn[0:65, avsl:avsl + 512],
                    v4_sb[g % 2][:, kc * 4 * 65 + j * 65: kc * 4 * 65 + j * 65 + 65],
                    pq_sb[pq_i][:, kk * 512:(kk + 1) * 512],
                    start=st, stop=sp), inc=(sT if kk == 1 else None))
            mAvP = p.now(sT)
            pqLastAv[pq_i] = mAvP
            if pr == npair - 1:
                mAvDone[(h, qc)] = mAvP
                epilogue(qc)

        for i in range(NPH):
            qc, pr, npair = pairs[i]
            if pr == 0 and (qc in (3, 5, 6, 7) or NQ == 1):
                pop_norm(2)
            stage_a(i)
            if i >= 2:
                stage_b(i - 2)
            pop_fill()
        stage_b(NPH - 2)
        stage_b(NPH - 1)
        W("vector", sZst[0], mZstore[(h, NQ - 1)])
        if h >= 1:
            W("vector", sRzQ[0], mRzQ[h - 1])
        def _recip(e):
            with nc.allow_low_precision(reason="1/Z bf16 broadcast operand"):
                return e.reciprocal(rz2[0:16 * NQ, :], zc2[0:16 * NQ, :])
        mRecip[h] = EM("vector", _recip, inc=sV)
        W("gpsimd", sV, mRecip[h])
        if h >= 1 and (h - 1, NQ - 1) in mNormBc:
            W("gpsimd", sT, mNormBc[(h - 1, NQ - 1)])
        mRzQ[h] = p.dma("gpsimd", sRzQ[0],
                        rzq[0:1, :].rearrange("p (q s) -> p q s", q=16 * NQ),
                        rz2[0:16 * NQ, :])
        for qc in range(NQ):
            normq.append(lambda h=h, qc=qc: norm_job(h, qc))
        mScAll[h] = p.now(sT)

    enqueue_head(0, with_loads=True)
    drain_fill()
    enqueue_head(1, with_loads=True)
    mW1re = None
    for h in range(H):
        wqk_load(h + 2)
        if (h + 2) % 4 == 0:
            wv_load((h + 2) // 4)
        if h == H - 1 and DT * S * 2 >= DT * FF * 2:
            W("sync", sV, mQKdone[h][0])
            W("sync", sT, mQKdone[h][1])
            W("sync", sV, mVG[NG - 1][0])
            W("sync", sT, mVG[NG - 1][1])
            W("sync", sWst, mW1img)
            HW_ = DT * FF // 2
            p.dma("sync", sTr[0], W1fT[:, 0:HW_], w1t_hbm[:, 0:HW_])
            mW1re = p.dma("sync", sTr[0], W1fT[:, HW_:2 * HW_],
                          w1t_hbm[:, HW_:2 * HW_])
        attn_head(h)
        drain_fill()
        enqueue_head(h + 2)
    W("gpsimd", sT, mAvDone[(H - 1, NQ - 1)])
    mWp = p.dma("gpsimd", sGw[0],
                wp_sb[:].rearrange("p (t d) -> p t d", t=DT),
                Wp.rearrange("(t p) d -> p t d", p=128))
    while normq:
        normq.pop(0)()
    p.barrier()

    W("sync", sWst2, mW2img)
    HW4 = FT * E // 2
    p.dma("sync", sTr[1], W2fT[:, 0:HW4], w2t_hbm[:, 0:HW4])
    mW2re = p.dma("sync", sTr[1], W2fT[:, HW4:2 * HW4], w2t_hbm[:, HW4:2 * HW4])
    if mW1re is None:
        W("sync", sWst, mW1img)
        HW_ = DT * FF // 2
        p.dma("sync", sTr[0], W1fT[:, 0:HW_], w1t_hbm[:, 0:HW_])
        mW1re = p.dma("sync", sTr[0], W1fT[:, HW_:2 * HW_],
                      w1t_hbm[:, HW_:2 * HW_])
    mA4, mX14, mE4, mAddP, mMM4, mAdd4, mSt4 = {}, {}, {}, {}, {}, {}, {}
    for c in range(NCH):
        if c >= 2:
            W("sync", sT, mMM4[c - 2])
        mA4[c] = p.dma("sync", sLa[c % 2],
                       a4c[c % 2][:].rearrange("p (t s) -> p t s", t=DT),
                       xchunk_ap(a_hbm, c * SC, SC))
        if c >= 2:
            W("sync", sV, mAdd4[c - 2])
        mX14[c] = p.dma("sync", sLx[c % 2],
                        x1c4[c % 2][:].rearrange("p (t s) -> p t s", t=DT),
                        xchunk_ap(x1_hbm, c * SC, SC))
        W("tensor", sLa[c % 2], mA4[c])
        W("tensor", sGw[0], mWp)
        for dch in range(DT):
            pb = PPm[:, (dch % 2) * 512:(dch % 2) * 512 + SC]
            if dch >= 2:
                W("tensor", sS, mE4[(c, dch - 2)])
            elif c >= 1:
                W("tensor", sS, mE4[(c - 1, DT - 2 + dch)])
            for t in range(DT):
                EM("tensor", lambda e, c=c, dch=dch, t=t, pb=pb: e.matmul(
                    pb, wp_sb[:, t * E + dch * 128: t * E + dch * 128 + 128],
                    a4c[c % 2][:, t * SC:(t + 1) * SC],
                    start=(t == 0), stop=(t == DT - 1)),
                    inc=(sT if t == DT - 1 else None))
            W("scalar", sT)
            if dch >= 2:
                W("scalar", sV, mAddP[(c, dch - 2)])
            elif c >= 1:
                W("scalar", sV, mAddP[(c - 1, DT - 2 + dch)])
            mE4[(c, dch)] = EM("scalar", lambda e, dch=dch, pb=pb: e.activation(
                e4c[dch % 2][:], pb, AF.Identity, bias=bp_sb[:, dch:dch + 1]), inc=sS)
            W("vector", sS, mE4[(c, dch)])
            W("vector", sLx[c % 2], mX14[c])
            if dch == 0 and c >= 2:
                W("vector", sSt[c % 2], mSt4[c - 2])
            mAddP[(c, dch)] = EM("vector", lambda e, c=c, dch=dch: e.tensor_add(
                x2oc[c % 2][:, dch * SC:(dch + 1) * SC], e4c[dch % 2][:],
                x1c4[c % 2][:, dch * SC:(dch + 1) * SC]), inc=sV)
        mMM4[c] = p.now(sT)
        mAdd4[c] = p.now(sV)
        W("gpsimd", sV, mAdd4[c])
        mSt4[c] = p.dma("gpsimd", sSt[c % 2],
                        xchunk_ap(x2_hbm, c * SC, SC),
                        x2oc[c % 2][:].rearrange("p (t s) -> p t s", t=DT))
    p.barrier()

    W("tensor", sTr[0], mW1re)

    def _p5_w2():
        W("tensor", sTr[1], mW2re)
    ffn_phase(x2_hbm, outT, pre_mm2=_p5_w2)

    for par in range(2):
        p.wait("gpsimd", sSt[par], p.now(sSt[par]))

    p.build()
    return nc


def kernel(L0_attn_x, L0_attn_W1, L0_attn_b1, L0_attn_Wp, L0_attn_bp,
           L0_ffn_W1, L0_ffn_b1, L0_ffn_W2, L0_ffn_b2):
    x = np.asarray(L0_attn_x, dtype=np.float32)
    B, S, _ = x.shape
    nc = build_nc(S)
    w = {
        "W1a": np.ascontiguousarray(np.asarray(L0_attn_W1, np.float32)),
        "b1a": np.ascontiguousarray(np.asarray(L0_attn_b1, np.float32)),
        "Wp": np.ascontiguousarray(np.asarray(L0_attn_Wp, np.float32)),
        "bp": np.ascontiguousarray(np.asarray(L0_attn_bp, np.float32)),
        "W1f": np.ascontiguousarray(np.asarray(L0_ffn_W1, np.float32)),
        "b1f": np.ascontiguousarray(np.asarray(L0_ffn_b1, np.float32)),
        "W2f": np.ascontiguousarray(np.asarray(L0_ffn_W2, np.float32)),
        "b2f": np.ascontiguousarray(np.asarray(L0_ffn_b2, np.float32)),
    }
    in_maps = []
    for b in range(B):
        m = dict(w)
        m["xT_in"] = np.ascontiguousarray(x[b].T)
        in_maps.append(m)
    res = run_bass_kernel_spmd(nc, in_maps, core_ids=list(range(B)))
    out = np.stack([np.asarray(res.results[b]["outT"]).T for b in range(B)])
    return np.ascontiguousarray(out)


# version 40
# speedup vs baseline: 1.1891x; 1.0071x over previous
import numpy as np
import ml_dtypes
import concourse.bass as bass
import concourse.mybir as mybir
from concourse.bass_utils import run_bass_kernel_spmd

F32 = mybir.dt.float32
BF16 = mybir.dt.bfloat16
AF = mybir.ActivationFunctionType
ALU = mybir.AluOpType
AX = mybir.AxisListType

ENGINES = ("sync", "scalar", "vector", "tensor", "gpsimd")

E = 1024
H = 16
FF = 4096
DT = 8
FT = 32
SC = 256
QCW = 512
KCW = 128


class Prog:

    def __init__(self, nc):
        self.nc = nc
        self.ops = {e: [] for e in ENGINES}
        self.count = {}
        self.waited = {}
        self.sems = {}

    def sem(self, name):
        h = self.nc.ctx.enter_context(self.nc.semaphore(name))
        self.count[name] = 0
        self.sems[name] = h
        return h

    def now(self, sem):
        return self.count[sem.name]

    def wait(self, eng, sem, val=None):
        if val is None:
            val = self.count[sem.name]
        assert val <= self.count[sem.name], (sem.name, val, self.count[sem.name])
        if val <= 0:
            return
        key = (eng, sem.name)
        if self.waited.get(key, 0) >= val:
            return
        self.waited[key] = val
        self.ops[eng].append(("wait", sem, val))

    def emit(self, eng, fn, inc=None, inc_n=1):
        if inc is not None:
            self.count[inc.name] += inc_n
        self.ops[eng].append(("op", fn, inc, inc_n))
        return self.count[inc.name] if inc is not None else None

    def dma(self, eng, sem, dst, src, **kw):
        self.wait(eng, sem, self.count[sem.name])
        return self.emit(eng, lambda e: e.dma_start(dst, src, **kw), inc=sem, inc_n=16)

    def barrier(self):
        for eng in ENGINES:
            for name, cnt in self.count.items():
                if cnt > 0:
                    key = (eng, name)
                    if self.waited.get(key, 0) < cnt:
                        self.waited[key] = cnt
                        self.ops[eng].append(("wait", self.sems[name], cnt))

    def build(self):
        nc = self.nc
        with nc.Block() as block:
            def mk(eng_name):
                def run(engine):
                    for item in self.ops[eng_name]:
                        if item[0] == "wait":
                            engine.wait_ge(item[1], item[2])
                        else:
                            _, fn, inc, inc_n = item
                            inst = fn(engine)
                            if inc is not None:
                                inst.then_inc(inc, inc_n)
                return run
            block.sync(mk("sync"))
            block.scalar(mk("scalar"))
            block.vector(mk("vector"))
            block.tensor(mk("tensor"))
            block.gpsimd(mk("gpsimd"))


def build_nc(S=4096):
    assert S % 512 == 0
    NCH = S // SC
    NQ = S // QCW
    NSS = S // KCW

    nc = bass.Bass("TRN2", target_bir_lowering=False)

    xT_in = nc.dram_tensor("xT_in", [E, S], F32, kind="ExternalInput").ap()
    W1a = nc.dram_tensor("W1a", [E, 3 * E], F32, kind="ExternalInput").ap()
    b1a = nc.dram_tensor("b1a", [3 * E], F32, kind="ExternalInput").ap()
    Wp = nc.dram_tensor("Wp", [E, E], F32, kind="ExternalInput").ap()
    bp = nc.dram_tensor("bp", [E], F32, kind="ExternalInput").ap()
    W1f = nc.dram_tensor("W1f", [FF, E], F32, kind="ExternalInput").ap()
    b1f = nc.dram_tensor("b1f", [FF], F32, kind="ExternalInput").ap()
    W2f = nc.dram_tensor("W2f", [E, FF], F32, kind="ExternalInput").ap()
    b2f = nc.dram_tensor("b2f", [E], F32, kind="ExternalInput").ap()
    outT = nc.dram_tensor("outT", [E, S], F32, kind="ExternalOutput").ap()

    x1_hbm = nc.dram_tensor("x1_hbm", [E, S], F32, kind="Internal").ap()
    x2_hbm = nc.dram_tensor("x2_hbm", [E, S], F32, kind="Internal").ap()
    a_hbm = nc.dram_tensor("a_hbm", [E, S], BF16, kind="Internal").ap()
    w1t_hbm = nc.dram_tensor("w1t_hbm", [128, DT * FF], BF16, kind="Internal").ap()
    w2t_hbm = nc.dram_tensor("w2t_hbm", [128, FT * E], BF16, kind="Internal").ap()

    def xchunk_ap(dram, c0, width):
        return dram.rearrange("(t p) s -> p t s", p=128)[:, :, c0:c0 + width]

    base = (nc.sbuf_base + 31) & ~31
    cap = nc.sbuf_base + nc.sbuf_bytes_remaining - base
    A_OFF = base
    B_OFF = base + 128 * 1024
    P_OFF = base + ((cap - 2560) & ~31)

    def at(name, shape, dt, off):
        t = nc.alloc_sbuf_tensor_at(name, shape, dt, offset=off)
        return t

    po = [P_OFF]

    def pat(name, shape, dt, size):
        t = at(name, shape, dt, po[0])
        po[0] += (size + 31) & ~31
        return t

    cst = pat("cst", [128, 8], F32, 32)
    ones_bf = pat("ones_bf", [128, 64], BF16, 128)
    ones_f = pat("ones_f", [128, 128], F32, 512)
    ident_f = pat("ident_f", [128, 128], F32, 512)
    ident_bf = pat("ident_bf", [128, 128], BF16, 256)
    b1f_sb = pat("b1f_sb", [128, 32], F32, 128)
    hb2f_sb = pat("hb2f_sb", [128, 8], F32, 32)
    bqk_sb = pat("bqk_sb", [128, 16], F32, 64)
    bp_sb = pat("bp_sb", [128, 8], F32, 32)
    bstage = pat("bstage", [32, 128], F32, 512)
    thr_sb = pat("thr_sb", [128, 4], F32, 32)
    scal_sb = pat("scal_sb", [1, 8], F32, 32)
    assert po[0] - base <= cap, po[0] - base

    W1fT = at("W1fT", [128, DT * FF], BF16, A_OFF)
    W2fT = at("W2fT", [128, FT * E], BF16, A_OFF + 64 * 1024)

    ao = A_OFF
    y2T = at("y2T", [128, DT * S], BF16, ao); ao += DT * S * 2
    qk_sb = [at("qk_sb0", [128, S], BF16, ao), at("qk_sb1", [128, S], BF16, ao + S * 2)]
    ao += 2 * S * 2
    kq_sb = [at("kq_sb0", [128, S], BF16, ao), at("kq_sb1", [128, S], BF16, ao + S * 2)]
    ao += 2 * S * 2
    v4a = at("v4a", [128, NSS * 4 * 65], BF16, ao); ao += NSS * 4 * 65 * 2
    pq_sb = [at(f"pq{i}", [128, 1024], BF16, ao + i * 2048) for i in range(5)]
    ao += 5 * 2048
    assert ao <= A_OFF + 128 * 1024, ao - A_OFF

    bo = B_OFF
    x0c = []
    for i in range(3):
        x0c.append(at(f"x0c{i}", [128, DT * SC], F32, bo)); bo += DT * SC * 4
    xbc = []
    for i in range(2):
        xbc.append(at(f"xbc{i}", [128, DT * SC], BF16, bo)); bo += DT * SC * 2
    xsqc = at("xsqc", [128, DT * SC], BF16, bo); bo += DT * SC * 2
    yc = []
    for i in range(2):
        yc.append(at(f"yc{i}", [128, DT * SC], BF16, bo)); bo += DT * SC * 2
    hc = at("hc", [128, FT * SC], BF16, bo); bo += FT * SC * 2
    x1oc = at("x1oc", [128, DT * SC], F32, bo); bo += DT * SC * 4
    ec = [at("ec0", [128, SC], F32, bo), at("ec1", [128, SC], F32, bo + SC * 4)]
    bo += 2 * SC * 4
    ntmp = [at(f"nt{i}", [128, SC], F32, bo + i * SC * 4) for i in range(3)]
    bo += 3 * SC * 4
    stat_m = at("stat_m", [1, SC], F32, bo); bo += SC * 4
    stat_v = at("stat_v", [1, SC], F32, bo); bo += SC * 4
    stat_sq = at("stat_sq", [1, SC], F32, bo); bo += SC * 4
    stat_m2 = at("stat_m2", [1, SC], F32, bo - SC * 4)
    stat_r = at("stat_r", [1, SC], F32, bo); bo += SC * 4
    assert bo <= P_OFF, bo - B_OFF

    po0 = B_OFF
    w1b = [at("w1b0", [128, FF], F32, po0), at("w1b1", [128, FF], F32, po0 + FF * 4)]
    po0 += 2 * FF * 4
    w2b = at("w2b", [128, FF], F32, po0); po0 += FF * 4
    w2c = at("w2c", [128, FF], F32, po0); po0 += FF * 4
    rnd = []
    for ri in range(6):
        a_ = at(f"rndA{ri}", [128, 512], BF16, po0); po0 += 512 * 2
        b_ = at(f"rndB{ri}", [128, 512], BF16, po0); po0 += 512 * 2
        rnd.append((a_, b_))
    parts = at("parts", [128, 36], F32, po0); po0 += 36 * 4
    assert po0 <= P_OFF

    po2 = B_OFF
    x1c2 = [at("x1c2_0", [128, DT * SC], F32, po2),
            at("x1c2_1", [128, DT * SC], F32, po2 + DT * SC * 4)]
    po2 += 2 * DT * SC * 4
    xbc2 = [at("xbc2_0", [128, DT * SC], BF16, po2),
            at("xbc2_1", [128, DT * SC], BF16, po2 + DT * SC * 2)]
    po2 += 2 * DT * SC * 2
    xsqc2 = at("xsqc2", [128, DT * SC], BF16, po2); po2 += DT * SC * 2

    pm = B_OFF + 28 * 1024
    mask_sb = at("mask_sb", [128, 2048], BF16, pm); pm += 4096
    bvrep_sb = at("bvrep_sb", [128, E], F32, pm); pm += E * 4
    p3o = pm
    ast_sb = [at(f"ast{i}", [64, QCW], BF16, p3o + i * QCW * 2) for i in range(4)]
    p3o += 4 * QCW * 2
    wqk_sb = [at("wqk0", [128, DT * 128], BF16, p3o),
              at("wqk1", [128, DT * 128], BF16, p3o + DT * 128 * 2)]
    p3o += 2 * DT * 128 * 2
    wv4_sb = at("wv4", [128, DT * 256], BF16, p3o); p3o += DT * 256 * 2
    v4b = at("v4b", [128, NSS * 4 * 65], BF16, p3o); p3o += NSS * 4 * 65 * 2
    ntmp2 = None
    assert p3o <= P_OFF
    pz = B_OFF
    zst = at("zst", [128, 512], F32, pz); pz += 512 * 4
    zc2 = at("zc2", [128, 32], F32, pz); pz += 32 * 4
    rz2 = at("rz2", [128, 32], BF16, pz); pz += 32 * 2 + 32
    astg = [at("astg0", [64, NQ * QCW], BF16, pz),
            at("astg1", [64, NQ * QCW], BF16, pz + NQ * QCW * 2)]
    pz += 2 * NQ * QCW * 2
    rz_sb = at("rz_sb", [NQ, 512], BF16, pz); pz += 512 * 2
    rzq = at("rzq", [1, NQ * 512], BF16, pz); pz += NQ * 512 * 2
    assert pz <= B_OFF + 28 * 1024, pz - B_OFF

    po4 = B_OFF
    a4c = [at("a4c0", [128, DT * SC], BF16, po4),
           at("a4c1", [128, DT * SC], BF16, po4 + DT * SC * 2)]
    po4 += 2 * DT * SC * 2
    x1c4 = [at("x1c4_0", [128, DT * SC], F32, po4),
            at("x1c4_1", [128, DT * SC], F32, po4 + DT * SC * 4)]
    po4 += 2 * DT * SC * 4
    x2oc = [at("x2oc0", [128, DT * SC], F32, po4),
            at("x2oc1", [128, DT * SC], F32, po4 + DT * SC * 4)]
    po4 += 2 * DT * SC * 4
    wp_sb = at("wp_sb", [128, DT * E], BF16, po4); po4 += DT * E * 2
    e4c = [at("e4c0", [128, SC], F32, po4), at("e4c1", [128, SC], F32, po4 + SC * 4)]
    po4 += 2 * SC * 4
    assert po4 <= P_OFF, po4 - B_OFF

    PPbig = nc.alloc_psum_tensor("PPbig", [128, 2048], F32)
    PPm = nc.alloc_psum_tensor("PPm", [128, 1024], F32)
    PPn = nc.alloc_psum_tensor("PPn", [128, 1024], F32)

    p = Prog(nc)
    sT = p.sem("sT"); sS = p.sem("sS"); sV = p.sem("sV"); sG = p.sem("sG")
    sLx = [p.sem("sLx0"), p.sem("sLx1"), p.sem("sLx2")]
    sLxb = [p.sem("sLxb0"), p.sem("sLxb1")]
    sLa = [p.sem("sLa0"), p.sem("sLa1")]
    sSt = [p.sem("sSt0"), p.sem("sSt1")]
    sSA = [p.sem("sSA0"), p.sem("sSA1")]
    sWst = p.sem("sWst")
    sTr = [p.sem("sTr0"), p.sem("sTr1")]
    sGw = [p.sem("sGw0"), p.sem("sGw1")]
    sZst = [p.sem("sZst0"), p.sem("sZst1")]
    sRzQ = [p.sem("sRzQ0"), p.sem("sRzQ1")]
    sW2L = [p.sem("sW2L0"), p.sem("sW2L1"), p.sem("sW2L2")]
    sWst2 = p.sem("sWst2")

    W = p.wait
    EM = p.emit

    for col, val in [(0, 1.0 / E), (1, -12.0), (2, 0.125), (3, 0.5),
                     (4, 1e-5), (5, 1.0 / (FF * E)), (6, 0.25)]:
        EM("gpsimd", lambda e, col=col, val=val: e.memset(cst[:, col:col + 1], val), inc=sG)
    EM("gpsimd", lambda e: e.memset(ones_bf[:], 1.0), inc=sG)
    EM("gpsimd", lambda e: e.memset(ones_f[:], 1.0), inc=sG)
    EM("gpsimd", lambda e: e.memset(ident_f[:], 0.0), inc=sG)
    EM("gpsimd", lambda e: e.memset(ident_bf[:], 0.0), inc=sG)
    W("gpsimd", sG)
    EM("gpsimd", lambda e: e.affine_select(
        out=ident_f[:], in_=ident_f[:], compare_op=ALU.not_equal,
        fill=1.0, base=0, pattern=[[-1, 128]], channel_multiplier=1), inc=sG)
    EM("gpsimd", lambda e: e.affine_select(
        out=ident_bf[:], in_=ident_bf[:], compare_op=ALU.not_equal,
        fill=1.0, base=0, pattern=[[-1, 128]], channel_multiplier=1), inc=sG)
    mPrepG = p.now(sG)

    def bias_transpose(src_ap, rows, dst, half):
        W("sync", sT)
        mL = p.dma("sync", sLx[0], bstage[0:rows, 0:128], src_ap)
        W("tensor", sLx[0], mL)
        W("tensor", sG, mPrepG)
        W("tensor", sS)
        EM("tensor", lambda e, rows=rows: e.transpose(
            PPm[0:128, 0:rows], bstage[0:rows, 0:128], ident_f[0:rows, 0:rows]), inc=sT)
        W("scalar", sT)
        if half:
            EM("scalar", lambda e, rows=rows, dst=dst: e.activation(
                dst[:, 0:rows], PPm[0:128, 0:rows], AF.Identity, scale=cst[:, 3:4]), inc=sS)
        else:
            EM("scalar", lambda e, rows=rows, dst=dst: e.activation(
                dst[:, 0:rows], PPm[0:128, 0:rows], AF.Identity), inc=sS)

    bias_transpose(b1f.rearrange("(t p) -> t p", p=128), 32, b1f_sb, False)
    bias_transpose(b2f.rearrange("(t p) -> t p", p=128), 8, hb2f_sb, True)
    bias_transpose(bp.rearrange("(t p) -> t p", p=128), 8, bp_sb, False)
    bias_transpose(b1a.rearrange("(h c) -> h c", c=192)[:, 0:128], 16, bqk_sb, False)

    def tern_threshold(pcol0, n_loads, thr_col, mP1, me="vector", ms=None,
                       ss_wait=None):
        if ms is None:
            ms = sV
        sb = 4 * thr_col
        tb = 2 * thr_col
        vece = me != "gpsimd"
        ev = "vector" if vece else "scalar"
        es = sV if vece else sS
        W("vector", sV, mP1)
        mTot = EM("vector", lambda e: e.reduce_sum(
            parts[:, 32 + thr_col:33 + thr_col],
            parts[:, pcol0:pcol0 + n_loads], AX.X), inc=sV)
        W("tensor", sV, mTot)
        W("tensor", sG, mPrepG)
        if ss_wait is not None:
            W("tensor", sS, ss_wait)
        mPs = EM("tensor", lambda e: e.matmul(
            PPn[0:1, thr_col:thr_col + 1], parts[:, 32 + thr_col:33 + thr_col],
            ones_f[:, 0:1], start=True, stop=True), inc=sT)
        W(ev, sT, mPs)
        if vece:
            mSc = EM(ev, lambda e: e.tensor_scalar(
                scal_sb[0:1, sb:sb + 1], PPn[0:1, thr_col:thr_col + 1],
                1.0 / (FF * E), None, ALU.mult), inc=es)
        else:
            mSc = EM(ev, lambda e: e.activation(
                scal_sb[0:1, sb:sb + 1], PPn[0:1, thr_col:thr_col + 1],
                AF.Identity, scale=cst[0:1, 5:6]), inc=es)
        W(me, es, mSc)
        EM(me, lambda e: e.tensor_scalar(
            scal_sb[0:1, sb + 1:sb + 2], scal_sb[0:1, sb:sb + 1], 1e-8, None,
            ALU.max), inc=ms)
        W(me, ms)
        EM(me, lambda e: e.tensor_scalar(
            scal_sb[0:1, sb + 2:sb + 3], scal_sb[0:1, sb + 1:sb + 2], 0.5, None,
            ALU.mult), inc=ms)
        W(me, ms)
        EM(me, lambda e: e.tensor_scalar(
            scal_sb[0:1, sb + 3:sb + 4], scal_sb[0:1, sb + 1:sb + 2], -0.5, None,
            ALU.mult), inc=ms)
        mThr = p.now(ms)
        W("tensor", ms, mThr)
        mBc = EM("tensor", lambda e: e.matmul(
            PPn[0:128, 512 + tb:514 + tb], ones_f[0:1, :],
            scal_sb[0:1, sb + 2:sb + 4], start=True, stop=True), inc=sT)
        W(ev, sT, mBc)
        if vece:
            mThrS = EM(ev, lambda e: e.tensor_copy(
                thr_sb[:, tb:tb + 2], PPn[0:128, 512 + tb:514 + tb]), inc=es)
        else:
            mThrS = EM(ev, lambda e: e.activation(
                thr_sb[:, tb:tb + 2], PPn[0:128, 512 + tb:514 + tb],
                AF.Identity), inc=es)
        return es, mThrS

    def w1f_src(ld):
        return W1f.rearrange("(b p) d -> p b d", p=128)[:, ld * 4:(ld + 1) * 4, :]

    def w1b_dst(buf):
        return buf[:].rearrange("p (a d) -> p a d", a=4)

    mRed1, mRed2, mLW2, mLd1 = {}, {}, {}, {}
    for ld in range(8):
        if ld >= 2:
            W("sync", sV, mRed1[ld - 2])
        mL = p.dma("sync", sLx[ld % 2], w1b_dst(w1b[ld % 2]), w1f_src(ld))
        mLd1[ld] = mL
        W("vector", sLx[ld % 2], mL)
        mRed1[ld] = EM("vector", lambda e, ld=ld: e.reduce_sum(
            parts[:, ld:ld + 1], w1b[ld % 2][:], AX.X,
            apply_absolute_value=True), inc=sV)
        if ld >= 2:
            W("scalar", sV, mRed2[ld - 2])
        w2buf = w2b if ld % 2 == 0 else w2c
        mLW2[ld] = p.dma("scalar", sW2L[0], w2buf[:],
                         W2f[ld * 128:(ld + 1) * 128, :])
        W("vector", sW2L[0], mLW2[ld])
        mRed2[ld] = EM("vector", lambda e, w2buf=w2buf, ld=ld: e.reduce_sum(
            parts[:, 16 + ld:17 + ld], w2buf[:], AX.X,
            apply_absolute_value=True), inc=sV)
    mP1a = p.now(sV)
    mThrS1 = tern_threshold(0, 8, 0, mP1a, me="gpsimd", ms=sG,
                            ss_wait=p.now(sS))[1]
    mP1b = p.now(sV)
    mThrS2 = tern_threshold(16, 8, 1, mP1b, me="gpsimd", ms=sG,
                            ss_wait=mThrS1)[1]

    bank1 = [PPbig[:, 0:512], PPbig[:, 512:1024], PPm[:, 0:512]]
    bank2 = [PPbig[:, 1024:1536], PPbig[:, 1536:2048], PPn[:, 0:512]]
    order1 = [6, 7, 0, 1, 2, 3, 4, 5]
    mTrG, mQG, mTrC = {}, {}, {}
    mTr2G, mQ2G, mTr2C = {}, {}, {}
    gi = 0
    gj = 0
    for i in range(8):
        ld = order1[i]
        buf = w1b[i % 2]
        if i >= 2:
            W("sync", sT, mTrC[order1[i - 2]])
            mLh = p.dma("sync", sLxb[i % 2],
                        w1b_dst(buf)[:, :, 0:512], w1f_src(ld)[:, :, 0:512])
            mL = p.dma("sync", sLx[i % 2],
                       w1b_dst(buf)[:, :, 512:1024], w1f_src(ld)[:, :, 512:1024])
            hsem = sLxb[i % 2]
        else:
            mLh = mL = mLd1[ld]
            hsem = sLx[i % 2]
        for t in range(DT):
            if t < 4:
                W("tensor", hsem, mLh)
            else:
                W("tensor", sLx[i % 2], mL)
            W("tensor", sG, mPrepG)
            W("tensor", sS, mThrS2)
            if gi >= 3:
                W("tensor", sG, mQG[gi - 3])
            pb1 = bank1[gi % 3]
            for a in range(4):
                EM("tensor", lambda e, t=t, a=a, pb1=pb1, buf=buf: e.transpose(
                    pb1[:, a * 128:(a + 1) * 128],
                    buf[:, a * E + t * 128: a * E + t * 128 + 128],
                    ident_f[:, :]), inc=(sT if a == 3 else None))
            mTrG[(ld, t)] = p.now(sT)
            rp = rnd[gi % 3]
            W("scalar", sT, mTrG[(ld, t)])
            W("scalar", sS, mThrS1)
            if gi >= 3:
                W("scalar", sG, mQG[gi - 3])
            EM("scalar", lambda e, pb1=pb1, rp=rp: e.activation(
                rp[0][:], pb1[:, 0:512], AF.Sign,
                bias=thr_sb[:, 1:2]), inc=sS)
            mSb = EM("scalar", lambda e, pb1=pb1, rp=rp: e.activation(
                rp[1][:], pb1[:, 0:512], AF.Sign,
                bias=thr_sb[:, 0:1]), inc=sS)
            W("gpsimd", sS, mSb)
            mQG[gi] = EM("gpsimd", lambda e, ld=ld, t=t, rp=rp: e.tensor_add(
                W1fT[:, t * FF + ld * 512: t * FF + ld * 512 + 512],
                rp[0][:], rp[1][:]), inc=sG)
            gi += 1
        mTrC[ld] = p.now(sT)
        ld2 = order1[i]
        buf2 = w2b if i % 2 == 0 else w2c
        if i >= 2:
            W("gpsimd", sT, mTr2C[order1[i - 2]])
            mL2h = p.dma("gpsimd", sW2L[2], buf2[:, 0:2048],
                         W2f[ld2 * 128:(ld2 + 1) * 128, 0:2048])
            mL2 = p.dma("gpsimd", sW2L[1], buf2[:, 2048:4096],
                        W2f[ld2 * 128:(ld2 + 1) * 128, 2048:4096])
            l2sem, l2hsem = sW2L[1], sW2L[2]
        else:
            mL2h = mL2 = mLW2[ld2]
            l2sem = l2hsem = sW2L[0]
        for g4 in range(FT // 4):
            if g4 < 4:
                W("tensor", l2hsem, mL2h)
            else:
                W("tensor", l2sem, mL2)
            W("tensor", sG, mPrepG)
            W("tensor", sS, mThrS2)
            if gj >= 3:
                W("tensor", sV, mQ2G[gj - 3])
            pb2 = bank2[gj % 3]
            for ff in range(4):
                f = g4 * 4 + ff
                EM("tensor", lambda e, f=f, ff=ff, pb2=pb2, buf2=buf2:
                   e.transpose(
                    pb2[:, ff * 128:(ff + 1) * 128],
                    buf2[:, f * 128:(f + 1) * 128],
                    ident_f[:, :]), inc=(sT if ff == 3 else None))
            mTr2G[gj] = p.now(sT)
            rp = rnd[3 + gj % 3]
            W("scalar", sT, mTr2G[gj])
            W("scalar", sS, mThrS2)
            if gj >= 3:
                W("scalar", sV, mQ2G[gj - 3])
            EM("scalar", lambda e, pb2=pb2, rp=rp: e.activation(
                rp[0][:], pb2[:, 0:512], AF.Sign,
                bias=thr_sb[:, 3:4]), inc=sS)
            mSb2 = EM("scalar", lambda e, pb2=pb2, rp=rp: e.activation(
                rp[1][:], pb2[:, 0:512], AF.Sign,
                bias=thr_sb[:, 2:3]), inc=sS)
            W("vector", sS, mSb2)
            for ff in range(4):
                f = g4 * 4 + ff
                EM("vector", lambda e, f=f, ld2=ld2, ff=ff, rp=rp:
                   e.tensor_add(
                    W2fT[:, f * E + ld2 * 128: f * E + ld2 * 128 + 128],
                    rp[0][:, ff * 128:(ff + 1) * 128],
                    rp[1][:, ff * 128:(ff + 1) * 128]), inc=(
                        sV if ff == 3 else None))
            mQ2G[gj] = p.now(sV)
            gj += 1
        mTr2C[ld2] = p.now(sT)
    mW1Done = p.now(sG)
    mW2Done = p.now(sV)
    mW1TrAll = p.now(sT)
    mW2TrAll = p.now(sT)

    W("gpsimd", sG, mW1Done)
    HW = DT * FF // 2
    p.dma("gpsimd", sWst, w1t_hbm[:, 0:HW], W1fT[:, 0:HW])
    mW1img = p.dma("gpsimd", sWst, w1t_hbm[:, HW:2 * HW], W1fT[:, HW:2 * HW])
    W("gpsimd", sV, mW2Done)
    p.dma("gpsimd", sWst2, w2t_hbm[:, 0:HW], W2fT[:, 0:HW])
    mW2img = p.dma("gpsimd", sWst2, w2t_hbm[:, HW:2 * HW], W2fT[:, HW:2 * HW])

    W("sync", sT, mW2TrAll)
    for eng_ in ("scalar", "vector"):
        W(eng_, sT, mW2TrAll)
        W(eng_, sV, mW2Done)
        W(eng_, sG, mW1Done)
        W(eng_, sS, p.now(sS))
    W("tensor", sG, mW1Done)
    W("tensor", sS, mThrS1)
    W("tensor", sS, mThrS2)

    def ffn_phase(x_in_dram, x_out_dram, pre_mm2=None):
        mX, mXb, mStats, mMV, mRstd, mBcast, mNorm = {}, {}, {}, {}, {}, {}, {}
        mGelu, mMM1, mMM2, mE, mAdd, mEpi, mStore = {}, {}, {}, {}, {}, {}, {}

        def load(c):
            if c >= NCH:
                return
            if c >= 3:
                W("sync", sV, mEpi[c - 3])
            mX[c] = p.dma("sync", sLx[c % 3],
                          x0c[c % 3][:].rearrange("p (t s) -> p t s", t=DT),
                          xchunk_ap(x_in_dram, c * SC, SC))

        def ln_front(c):
            if c >= NCH:
                return
            W("scalar", sLx[c % 3], mX[c])
            if c >= 2:
                W("scalar", sT, mStats[c - 2])
                W("scalar", sV, mNorm[c - 2])
            EM("scalar", lambda e, c=c: e.copy(xbc[c % 2][:], x0c[c % 3][:]), inc=sS)
            if c >= 1:
                W("scalar", sT, mStats[c - 1])
            W("scalar", sS)
            mXb[c] = EM("scalar", lambda e, c=c: e.square(xsqc[:], xbc[c % 2][:]), inc=sS)
            W("tensor", sS, mXb[c])
            if c >= 1:
                W("tensor", sS, mMV[c - 1])
            for t in range(DT):
                EM("tensor", lambda e, c=c, t=t: e.matmul(
                    PPm[0:1, 0:SC], ones_bf[:, 0:1],
                    xbc[c % 2][:, t * SC:(t + 1) * SC],
                    start=(t == 0), stop=(t == DT - 1)),
                    inc=(sT if t == DT - 1 else None))
            for t in range(DT):
                EM("tensor", lambda e, t=t: e.matmul(
                    PPm[0:1, 512:512 + SC], ones_bf[:, 0:1],
                    xsqc[:, t * SC:(t + 1) * SC],
                    start=(t == 0), stop=(t == DT - 1)),
                    inc=(sT if t == DT - 1 else None))
            mStats[c] = p.now(sT)

        def ln_mid(c):
            if c >= NCH:
                return
            W("scalar", sT, mStats[c])
            if c >= 1:
                W("scalar", sT, mBcast[c - 1])
            EM("scalar", lambda e: e.activation(
                stat_m[0:1, 0:SC], PPm[0:1, 0:SC], AF.Identity,
                scale=cst[0:1, 0:1]), inc=sS)
            W("scalar", sS)
            EM("scalar", lambda e: e.square(stat_m2[0:1, 0:SC], stat_m[0:1, 0:SC]),
               inc=sS)
            EM("scalar", lambda e: e.activation(
                stat_v[0:1, 0:SC], PPm[0:1, 512:512 + SC], AF.Identity,
                scale=cst[0:1, 0:1]), inc=sS)
            mMV[c] = p.now(sS)
            W("vector", sS, mMV[c])
            if c >= 1:
                W("vector", sT, mBcast[c - 1])
            EM("vector", lambda e: e.tensor_sub(
                stat_v[0:1, 0:SC], stat_v[0:1, 0:SC], stat_m2[0:1, 0:SC]), inc=sV)
            W("scalar", sV)
            EM("scalar", lambda e: e.activation(
                stat_sq[0:1, 0:SC], stat_v[0:1, 0:SC], AF.Sqrt,
                bias=cst[0:1, 4:5]), inc=sS)
            W("vector", sS)
            mRstd[c] = EM("vector", lambda e: e.reciprocal(
                stat_r[0:1, 0:SC], stat_sq[0:1, 0:SC]), inc=sV)

        def ln_bcast(c):
            if c >= NCH:
                return
            W("tensor", sV, mRstd[c])
            if c >= 1:
                W("tensor", sV, mNorm[c - 1])
            EM("tensor", lambda e: e.matmul(
                PPn[:, 0:SC], ones_f[0:1, :], stat_m[0:1, 0:SC],
                start=True, stop=True), inc=sT)
            EM("tensor", lambda e: e.matmul(
                PPn[:, 512:512 + SC], ones_f[0:1, :], stat_r[0:1, 0:SC],
                start=True, stop=True), inc=sT)
            mBcast[c] = p.now(sT)

        def ln_norm(c):
            if c >= NCH:
                return
            W("vector", sT, mBcast[c])
            if c >= 2:
                W("vector", sS, mMM1[c - 2])
            mSubT, mMulT = {}, {}
            t = 0
            for gsz in (3, 3, 2):
                for i in range(gsz):
                    tt_ = t + i
                    if tt_ >= 3:
                        W("vector", sV, mMulT[tt_ - 3])
                    mSubT[tt_] = EM("vector", lambda e, c=c, tt_=tt_, i2=tt_ % 3:
                                    e.tensor_sub(
                        ntmp[i2][:], xbc[c % 2][:, tt_ * SC:(tt_ + 1) * SC],
                        PPn[:, 0:SC]), inc=sV)
                for i in range(gsz):
                    tt_ = t + i
                    W("vector", sV, mSubT[tt_])
                    mMulT[tt_] = EM("vector", lambda e, c=c, tt_=tt_, i2=tt_ % 3:
                                    e.tensor_mul(
                        yc[c % 2][:, tt_ * SC:(tt_ + 1) * SC], ntmp[i2][:],
                        PPn[:, 512:512 + SC]), inc=sV)
                t += gsz
            mNorm[c] = p.now(sV)

        def mm1(c):
            W("tensor", sV, mNorm[c])
            for f in range(FT):
                pb = PPbig[:, (f % 2) * 512:(f % 2) * 512 + SC]
                if f >= 2:
                    W("tensor", sS, mGelu[(c, f - 2)])
                elif c >= 1:
                    W("tensor", sS, mGelu[(c - 1, FT - 2 + f)])
                for t in range(DT):
                    EM("tensor", lambda e, c=c, f=f, t=t, pb=pb: e.matmul(
                        pb, W1fT[:, t * FF + f * 128: t * FF + f * 128 + 128],
                        yc[c % 2][:, t * SC:(t + 1) * SC],
                        start=(t == 0), stop=(t == DT - 1)),
                        inc=(sT if t == DT - 1 else None))
                W("scalar", sT)
                if c >= 1 and f == 0:
                    W("scalar", sT, mMM2[c - 1])
                mGelu[(c, f)] = EM("scalar", lambda e, f=f, pb=pb: e.activation(
                    hc[:, f * SC:(f + 1) * SC], pb, AF.Gelu,
                    bias=b1f_sb[:, f:f + 1], scale=cst[:, 3:4]), inc=sS)
            mMM1[c] = p.now(sS)

        def mm2_epi(c):
            if c == 0 and pre_mm2 is not None:
                pre_mm2()
            for dch in range(DT):
                pb = PPbig[:, 1024 + (dch % 2) * 512:1024 + (dch % 2) * 512 + SC]
                if dch >= 2:
                    W("tensor", sS, mE[(c, dch - 2)])
                elif c >= 1:
                    W("tensor", sS, mE[(c - 1, DT - 2 + dch)])
                for f in range(FT):
                    W("tensor", sS, mGelu[(c, f)])
                    EM("tensor", lambda e, c=c, f=f, dch=dch, pb=pb: e.matmul(
                        pb, W2fT[:, f * E + dch * 128: f * E + dch * 128 + 128],
                        hc[:, f * SC:(f + 1) * SC],
                        start=(f == 0), stop=(f == FT - 1)),
                        inc=(sT if f == FT - 1 else None))
                W("scalar", sT)
                if dch >= 2:
                    W("scalar", sV, mAdd[(c, dch - 2)])
                elif c >= 1:
                    W("scalar", sV, mAdd[(c - 1, DT - 2 + dch)])
                mE[(c, dch)] = EM("scalar", lambda e, dch=dch, pb=pb: e.activation(
                    ec[dch % 2][:], pb, AF.Identity,
                    bias=hb2f_sb[:, dch:dch + 1], scale=cst[:, 6:7]), inc=sS)
                W("vector", sS, mE[(c, dch)])
                if dch == 0 and c >= 1:
                    W("vector", sSt[(c - 1) % 2], mStore[c - 1])
                mAdd[(c, dch)] = EM("vector", lambda e, c=c, dch=dch: e.tensor_add(
                    x1oc[:, dch * SC:(dch + 1) * SC], ec[dch % 2][:],
                    x0c[c % 3][:, dch * SC:(dch + 1) * SC]), inc=sV)
            mMM2[c] = p.now(sT)
            mEpi[c] = p.now(sV)
            W("gpsimd", sV, mEpi[c])
            mStore[c] = p.dma("gpsimd", sSt[c % 2],
                              xchunk_ap(x_out_dram, c * SC, SC),
                              x1oc[:].rearrange("p (t s) -> p t s", t=DT))

        load(0); load(1); load(2)
        ln_front(0); ln_mid(0); ln_bcast(0); ln_norm(0)
        ln_front(1); ln_mid(1)
        for c in range(NCH):
            mm1(c)
            ln_bcast(c + 1); ln_norm(c + 1)
            ln_front(c + 2); ln_mid(c + 2)
            mm2_epi(c)
            load(c + 3)

    def _w2ft_ready():
        W("tensor", sV, mW2Done)
    ffn_phase(xT_in, x1_hbm, pre_mm2=_w2ft_ready)
    p.barrier()

    EM("gpsimd", lambda e: e.memset(mask_sb[:], 0.0), inc=sG)
    W("gpsimd", sG)
    for o_i, o in enumerate((0, 128, 256, 384)):
        EM("gpsimd", lambda e, o_i=o_i, o=o: e.affine_select(
            out=mask_sb[:, o_i * 512:(o_i + 1) * 512],
            in_=mask_sb[:, o_i * 512:(o_i + 1) * 512],
            compare_op=ALU.is_ge, fill=1.0, base=o - 1,
            pattern=[[-1, 512]], channel_multiplier=1), inc=sG)
    mMasks = p.now(sG)
    mBv = p.dma("sync", sLa[0],
                bvrep_sb[0:1, :].rearrange("p (h c) -> p h c", h=H),
                b1a.rearrange("(h c) -> h c", c=192)[:, 128:192])
    W("tensor", sLa[0], mBv)
    EM("tensor", lambda e: e.matmul(
        PPm[:, 0:512], ones_f[0:1, :], bvrep_sb[0:1, 0:512],
        start=True, stop=True), inc=sT)
    EM("tensor", lambda e: e.matmul(
        PPm[:, 512:1024], ones_f[0:1, :], bvrep_sb[0:1, 512:1024],
        start=True, stop=True), inc=sT)
    W("scalar", sT)
    mBvRep = EM("scalar", lambda e: e.activation(
        bvrep_sb[:, :], PPm[:, 0:1024], AF.Identity), inc=sS)

    mX2, mXb2, mSt2, mMV2, mRs2, mBc2, mN2 = {}, {}, {}, {}, {}, {}, {}

    def ln2_load(c):
        if c >= NCH:
            return
        if c >= 2:
            W("sync", sS, mXb2[c - 2])
        mX2[c] = p.dma("sync", sLx[c % 2],
                       x1c2[c % 2][:].rearrange("p (t s) -> p t s", t=DT),
                       xchunk_ap(x1_hbm, c * SC, SC))

    def ln2_front(c):
        if c >= NCH:
            return
        W("scalar", sLx[c % 2], mX2[c])
        if c >= 2:
            W("scalar", sT, mSt2[c - 2])
            W("scalar", sV, mN2[c - 2])
        EM("scalar", lambda e, c=c: e.copy(xbc2[c % 2][:], x1c2[c % 2][:]), inc=sS)
        if c >= 1:
            W("scalar", sT, mSt2[c - 1])
        W("scalar", sS)
        mXb2[c] = EM("scalar", lambda e, c=c: e.square(xsqc2[:], xbc2[c % 2][:]), inc=sS)
        W("tensor", sS, mXb2[c])
        stp = PPm if c % 2 == 0 else PPn
        if c >= 2:
            W("tensor", sS, mMV2[c - 2])
        for t in range(DT):
            EM("tensor", lambda e, c=c, t=t, stp=stp: e.matmul(
                stp[0:1, 0:SC], ones_bf[:, 0:1],
                xbc2[c % 2][:, t * SC:(t + 1) * SC],
                start=(t == 0), stop=(t == DT - 1)), inc=(sT if t == DT - 1 else None))
        for t in range(DT):
            EM("tensor", lambda e, t=t, stp=stp: e.matmul(
                stp[0:1, 512:512 + SC], ones_bf[:, 0:1],
                xsqc2[:, t * SC:(t + 1) * SC],
                start=(t == 0), stop=(t == DT - 1)), inc=(sT if t == DT - 1 else None))
        mSt2[c] = p.now(sT)

    def ln2_mid(c):
        if c >= NCH:
            return
        W("scalar", sT, mSt2[c])
        stp = PPm if c % 2 == 0 else PPn
        if c >= 1:
            W("scalar", sT, mBc2[c - 1])
        EM("scalar", lambda e, stp=stp: e.activation(
            stat_m[0:1, 0:SC], stp[0:1, 0:SC], AF.Identity, scale=cst[0:1, 0:1]), inc=sS)
        W("scalar", sS)
        EM("scalar", lambda e: e.square(stat_m2[0:1, 0:SC], stat_m[0:1, 0:SC]), inc=sS)
        EM("scalar", lambda e, stp=stp: e.activation(
            stat_v[0:1, 0:SC], stp[0:1, 512:512 + SC], AF.Identity,
            scale=cst[0:1, 0:1]), inc=sS)
        mMV2[c] = p.now(sS)
        W("vector", sS, mMV2[c])
        if c >= 1:
            W("vector", sT, mBc2[c - 1])
        EM("vector", lambda e: e.tensor_sub(
            stat_v[0:1, 0:SC], stat_v[0:1, 0:SC], stat_m2[0:1, 0:SC]), inc=sV)
        W("scalar", sV)
        EM("scalar", lambda e: e.activation(
            stat_sq[0:1, 0:SC], stat_v[0:1, 0:SC], AF.Sqrt, bias=cst[0:1, 4:5]), inc=sS)
        W("vector", sS)
        mRs2[c] = EM("vector", lambda e: e.reciprocal(
            stat_r[0:1, 0:SC], stat_sq[0:1, 0:SC]), inc=sV)

    def ln2_back(c):
        if c >= NCH:
            return
        W("tensor", sV, mRs2[c])
        bq = 0 if c % 2 == 0 else 1024
        if c >= 2:
            W("tensor", sV, mN2[c - 2])
        EM("tensor", lambda e, bq=bq: e.matmul(
            PPbig[:, bq:bq + SC], ones_f[0:1, :], stat_m[0:1, 0:SC],
            start=True, stop=True), inc=sT)
        EM("tensor", lambda e, bq=bq: e.matmul(
            PPbig[:, bq + 512:bq + 512 + SC], ones_f[0:1, :], stat_r[0:1, 0:SC],
            start=True, stop=True), inc=sT)
        mBc2[c] = p.now(sT)
        W("vector", sT, mBc2[c])
        mSubT, mMulT = {}, {}
        t = 0
        for gsz in (3, 3, 2):
            for i in range(gsz):
                tt_ = t + i
                if tt_ >= 3:
                    W("vector", sV, mMulT[tt_ - 3])
                mSubT[tt_] = EM("vector", lambda e, c=c, tt_=tt_, i2=tt_ % 3, bq=bq:
                                e.tensor_sub(
                    ntmp[i2][:], xbc2[c % 2][:, tt_ * SC:(tt_ + 1) * SC],
                    PPbig[:, bq:bq + SC]), inc=sV)
            for i in range(gsz):
                tt_ = t + i
                W("vector", sV, mSubT[tt_])
                mMulT[tt_] = EM("vector", lambda e, c=c, tt_=tt_, i2=tt_ % 3, bq=bq:
                                e.tensor_mul(
                    y2T[:, tt_ * S + c * SC: tt_ * S + c * SC + SC],
                    ntmp[i2][:], PPbig[:, bq + 512:bq + 512 + SC]), inc=sV)
            t += gsz
        mN2[c] = p.now(sV)

    ln2_load(0); ln2_load(1)
    ln2_front(0); ln2_mid(0)
    for c in range(NCH):
        ln2_load(c + 2)
        ln2_back(c)
        ln2_front(c + 1); ln2_mid(c + 1)
    p.barrier()

    NG = H // 4
    v4_sb = [v4a, v4b]
    mWqk, mWv, mVG, mQKdone, mDup = {}, {}, {}, {}, {}
    mScAll, mAvDone, mAstStore = {}, {}, {}
    mEvacPair, mZstore, mRecip, mRzQ = {}, {}, {}, {}
    mNormBc, mNormMult = {}, {}
    lastAvHalf = {}
    lastExpOn = {}
    pqLastAv = {}
    lastFillEvac = [None, None]
    lastNormBc = [None, None]
    lastNormMult = [None, None]
    lastZst = [None, None]
    fillq = []
    normq = []

    for vb in range(2):
        EM("gpsimd", lambda e, vb=vb: e.memset(
            v4_sb[vb][:].rearrange("p (ss j c) -> p ss j c", j=4, c=65)[:, :, :, 64:65],
            1.0), inc=sG)
    mOnesV = p.now(sG)

    def wqk_load(h):
        if h >= H:
            return
        if h >= 2 and (h - 2) in mQKdone:
            W("gpsimd", sT, mQKdone[h - 2][1])
        src_ = W1a.rearrange("(t p) f -> p t f", p=128)[:, :, h * 192: h * 192 + 128]
        mWqk[h] = p.dma("gpsimd", sGw[h % 2],
                        wqk_sb[h % 2][:].rearrange("p (t f) -> p t f", t=DT), src_)

    def wv_load(g):
        if g >= NG:
            return
        if g >= 1:
            W("gpsimd", sT, mVG[g - 1][1])
        for j in range(4):
            hh = g * 4 + j
            src_ = W1a.rearrange("(t p) f -> p t f", p=128)[
                :, :, hh * 192 + 128: hh * 192 + 192]
            dst = wv4_sb[:].rearrange("p (t f) -> p t f", t=DT)[:, :, j * 64:(j + 1) * 64]
            mWv[g] = p.dma("gpsimd", sGw[g % 2], dst, src_)

    def wait_fill_bank(eng):
        kind, m = lastFillEvac
        if m is not None:
            W(eng, sS if kind == "S" else sV, m)

    def fill_v(g, ss):
        W("tensor", sGw[g % 2], mWv[g])
        W("tensor", sG, mOnesV)
        if g >= 2:
            W("tensor", sT, mScAll.get((g - 2) * 4 + 3, 0) or 0)
        wait_fill_bank("tensor")
        for t in range(DT):
            EM("tensor", lambda e, ss=ss, t=t: e.matmul(
                PPm[:, 512:768],
                y2T[:, t * S + ss * 128: t * S + ss * 128 + 128],
                wv4_sb[:, t * 256:(t + 1) * 256],
                start=(t == 0), stop=(t == DT - 1)),
                inc=(sT if t == DT - 1 else None))
        mV = p.now(sT)
        W("vector", sT, mV)
        W("vector", sS, mBvRep)
        mE_ = EM("vector", lambda e, g=g, ss=ss: e.tensor_add(
            v4_sb[g % 2][:].rearrange("p (s2 j c) -> p s2 j c", j=4, c=65)[
                :, ss, :, 0:64],
            PPm[:, 512:768], bvrep_sb[:, g * 256:(g + 1) * 256]), inc=sV)
        lastFillEvac[0], lastFillEvac[1] = "V", mE_
        if ss == NSS - 1:
            mVG[g] = (p.now(sV), p.now(sT))

    def fill_qk(h, qs):
        W("tensor", sGw[h % 2], mWqk[h])
        if h >= 2 and (h - 2) in mScAll:
            W("tensor", sT, mScAll[h - 2])
        wait_fill_bank("tensor")
        for t in range(DT):
            EM("tensor", lambda e, h=h, qs=qs, t=t: e.matmul(
                PPm[0:128, 512:1024],
                wqk_sb[h % 2][:, t * 128: t * 128 + 128],
                y2T[:, t * S + qs * 512: t * S + qs * 512 + 512],
                start=(t == 0), stop=(t == DT - 1)),
                inc=(sT if t == DT - 1 else None))
        W("vector", sT)
        mC = EM("vector", lambda e, h=h, qs=qs: e.tensor_scalar(
            qk_sb[h % 2][0:128, qs * 512:(qs + 1) * 512], PPm[0:128, 512:1024],
            bqk_sb[:, h:h + 1], None, ALU.add), inc=sV)
        lastFillEvac[0], lastFillEvac[1] = "V", mC
        halves = []
        if qs == max(NQ // 2 - 1, 0):
            halves.append(0)
        if qs == NQ - 1:
            halves.append(1)
        for half in halves:
            c0, c1 = half * (S // 2), (half + 1) * (S // 2)
            sdp = sLa[half]
            W("sync", sV, mC)
            if h >= 2 and (h - 2) in mScAll:
                W("sync", sT, mScAll[h - 2])
            p.dma("sync", sdp, kq_sb[h % 2][64:128, c0:c1],
                  qk_sb[h % 2][0:64, c0:c1])
            mDup[(h, half)] = p.dma("sync", sdp, kq_sb[h % 2][0:64, c0:c1],
                                    qk_sb[h % 2][64:128, c0:c1])
        if qs == NQ - 1:
            mQKdone[h] = (p.now(sV), p.now(sT))

    def enqueue_head(h, with_loads=False):
        if h >= H:
            return
        if with_loads:
            wqk_load(h)
            if h % 4 == 0:
                wv_load(h // 4)
        if h % 4 == 0:
            for ss in range(NSS):
                fillq.append(lambda g=h // 4, ss=ss: fill_v(g, ss))
        for qs in range(NQ):
            fillq.append(lambda h=h, qs=qs: fill_qk(h, qs))

    def pop_fill():
        if fillq:
            fillq.pop(0)()

    def drain_fill():
        while fillq:
            fillq.pop(0)()

    def norm_job(h, qc):
        kk_ = h * NQ + qc
        slot = kk_ % 2
        r0 = slot * 64
        W("tensor", sRzQ[0], mRzQ[h])
        if lastNormMult[slot] is not None:
            W("tensor", sV, lastNormMult[slot])
        mNormBc[(h, qc)] = EM("tensor", lambda e, qc=qc, r0=r0: e.matmul(
            PPm[r0:r0 + 64, 0:512], ones_bf[0:1, 0:64],
            rzq[0:1, qc * 512:(qc + 1) * 512],
            start=True, stop=True), inc=sT)
        lastNormBc[0] = mNormBc[(h, qc)]
        W("vector", sT, mNormBc[(h, qc)])
        if kk_ >= 4:
            W("vector", sSA[kk_ % 2], mAstStore[kk_ - 4])
        mNormMult[(h, qc)] = EM("vector", lambda e, h=h, qc=qc, kk_=kk_, r0=r0:
                                e.tensor_mul(
            ast_sb[kk_ % 4][0:64, 0:512],
            astg[h % 2][:, qc * 512:(qc + 1) * 512],
            PPm[r0:r0 + 64, 0:512]), inc=sV)
        lastNormMult[slot] = mNormMult[(h, qc)]
        W("sync", sV, mNormMult[(h, qc)])
        mAstStore[kk_] = p.dma("sync", sSA[kk_ % 2],
                               a_hbm[h * 64:(h + 1) * 64,
                                     qc * 512:(qc + 1) * 512],
                               ast_sb[kk_ % 4][0:64, 0:512])

    def pop_norm(n):
        for _ in range(n):
            if normq:
                normq.pop(0)()

    def attn_head(h):
        g = h // 4
        j = h % 4
        qksb, kqsb = qk_sb[h % 2], kq_sb[h % 2]
        pairs = []
        for qc in range(NQ):
            npair = 2 * (qc + 1)
            for pr in range(npair):
                pairs.append((qc, pr, npair))
        NPH = len(pairs)
        mExpP, mMskP, prPq = {}, {}, {}

        def epilogue(qc):
            avsl = (qc % 2) * 512
            W("vector", sT, mAvDone[(h, qc)])
            EM("vector", lambda e, h=h, qc=qc, avsl=avsl: e.tensor_copy(
                astg[h % 2][:, qc * 512:(qc + 1) * 512],
                PPn[0:64, avsl:avsl + 512]), inc=sV)
            if lastZst[0] is not None:
                W("vector", sZst[0], lastZst[0])
            EM("vector", lambda e, avsl=avsl: e.tensor_copy(
                zst[64:65, 0:512], PPn[64:65, avsl:avsl + 512]), inc=sV)
            mEvacPair[(h, qc)] = p.now(sV)
            W("gpsimd", sV, mEvacPair[(h, qc)])
            if qc == 0 and h >= 1:
                W("gpsimd", sV, mRecip[h - 1])
            mZstore[(h, qc)] = p.dma(
                "gpsimd", sZst[0], zc2[qc * 16:(qc + 1) * 16, :],
                zst[64:65, 0:512].rearrange("p (a s) -> p a s", a=16))
            lastZst[0] = mZstore[(h, qc)]

        def stage_a(i):
            qc, pr, npair = pairs[i]
            kc0 = 2 * pr
            sc0 = (i % 2) * 1024
            le = lastExpOn.get(i % 2)
            if le is not None:
                W("tensor", sS, le)
            W("tensor", sV, mQKdone[h][0])
            W("tensor", sT, mQKdone[h][1])
            W("tensor", sLa[0], mDup[(h, 0)])
            if (qc + 1) * 512 > S // 2 or (kc0 + 2) * 128 > S // 2:
                W("tensor", sLa[1], mDup[(h, 1)])
            W("tensor", sV, mVG[g][0])
            W("tensor", sT, mVG[g][1])
            for rt in range(2):
                kc = kc0 + rt
                r0 = rt * 64
                ktile = kqsb if rt == 0 else qksb
                qtile = qksb if rt == 0 else kqsb
                EM("tensor", lambda e, qc=qc, kc=kc, rt=rt, r0=r0, sc0=sc0,
                   ktile=ktile, qtile=qtile: e.matmul(
                    PPbig[0:128, sc0 + rt * 512: sc0 + rt * 512 + 512],
                    ktile[r0:r0 + 64, kc * 128:(kc + 1) * 128],
                    qtile[r0:r0 + 64, qc * 512:(qc + 1) * 512],
                    start=True, stop=True), inc=(sT if rt == 1 else None))
            mScP = p.now(sT)
            pq_i = i % 5
            prPq[i] = pq_i
            W("scalar", sT, mScP)
            la = pqLastAv.get(pq_i)
            if la is not None:
                W("scalar", sT, la)
            mExpP[i] = EM("scalar", lambda e, pq_i=pq_i, sc0=sc0: e.activation(
                pq_sb[pq_i][:, 0:1024], PPbig[:, sc0:sc0 + 1024], AF.Exp,
                bias=cst[:, 1:2], scale=cst[:, 2:3]), inc=sS)
            lastExpOn[i % 2] = mExpP[i]
            dpr = pr - (npair - 2)
            if dpr >= 0:
                W("vector", sS, mExpP[i])
                mMskP[i] = EM("vector", lambda e, pq_i=pq_i, dpr=dpr: e.tensor_mul(
                    pq_sb[pq_i][:, 0:1024], pq_sb[pq_i][:, 0:1024],
                    mask_sb[:, dpr * 1024:(dpr + 1) * 1024]), inc=sV)

        def stage_b(i):
            qc, pr, npair = pairs[i]
            kc0 = 2 * pr
            nk = 2 * npair
            avsl = (qc % 2) * 512
            pq_i = prPq[i]
            if pr == 0:
                prev = lastAvHalf.get(qc % 2)
                if prev is not None:
                    W("tensor", sV, mEvacPair[prev])
                lastAvHalf[qc % 2] = (h, qc)
            if i in mMskP:
                W("tensor", sV, mMskP[i])
            else:
                W("tensor", sS, mExpP[i])
            for kk in range(2):
                kc = kc0 + kk
                EM("tensor", lambda e, kc=kc, kk=kk, pq_i=pq_i, j=j, avsl=avsl,
                   st=(kc0 + kk == 0), sp=(kc0 + kk == nk - 1): e.matmul(
                    PPn[0:65, avsl:avsl + 512],
                    v4_sb[g % 2][:, kc * 4 * 65 + j * 65: kc * 4 * 65 + j * 65 + 65],
                    pq_sb[pq_i][:, kk * 512:(kk + 1) * 512],
                    start=st, stop=sp), inc=(sT if kk == 1 else None))
            mAvP = p.now(sT)
            pqLastAv[pq_i] = mAvP
            if pr == npair - 1:
                mAvDone[(h, qc)] = mAvP
                epilogue(qc)

        for i in range(NPH):
            qc, pr, npair = pairs[i]
            if pr == 0 and (qc in (3, 5, 6, 7) or NQ == 1):
                pop_norm(2)
            stage_a(i)
            if i >= 2:
                stage_b(i - 2)
            pop_fill()
        stage_b(NPH - 2)
        stage_b(NPH - 1)
        W("vector", sZst[0], mZstore[(h, NQ - 1)])
        if h >= 1:
            W("vector", sRzQ[0], mRzQ[h - 1])
        def _recip(e):
            with nc.allow_low_precision(reason="1/Z bf16 broadcast operand"):
                return e.reciprocal(rz2[0:16 * NQ, :], zc2[0:16 * NQ, :])
        mRecip[h] = EM("vector", _recip, inc=sV)
        W("gpsimd", sV, mRecip[h])
        if h >= 1 and (h - 1, NQ - 1) in mNormBc:
            W("gpsimd", sT, mNormBc[(h - 1, NQ - 1)])
        mRzQ[h] = p.dma("gpsimd", sRzQ[0],
                        rzq[0:1, :].rearrange("p (q s) -> p q s", q=16 * NQ),
                        rz2[0:16 * NQ, :])
        for qc in range(NQ):
            normq.append(lambda h=h, qc=qc: norm_job(h, qc))
        mScAll[h] = p.now(sT)

    enqueue_head(0, with_loads=True)
    drain_fill()
    enqueue_head(1, with_loads=True)
    mW1re = None
    for h in range(H):
        wqk_load(h + 2)
        if (h + 2) % 4 == 0:
            wv_load((h + 2) // 4)
        if h == H - 1 and DT * S * 2 >= DT * FF * 2:
            W("sync", sV, mQKdone[h][0])
            W("sync", sT, mQKdone[h][1])
            W("sync", sV, mVG[NG - 1][0])
            W("sync", sT, mVG[NG - 1][1])
            W("sync", sWst, mW1img)
            HW_ = DT * FF // 2
            p.dma("sync", sTr[0], W1fT[:, 0:HW_], w1t_hbm[:, 0:HW_])
            mW1re = p.dma("sync", sTr[0], W1fT[:, HW_:2 * HW_],
                          w1t_hbm[:, HW_:2 * HW_])
        attn_head(h)
        drain_fill()
        enqueue_head(h + 2)
    W("gpsimd", sT, mAvDone[(H - 1, NQ - 1)])
    mWp = p.dma("gpsimd", sGw[0],
                wp_sb[:].rearrange("p (t d) -> p t d", t=DT),
                Wp.rearrange("(t p) d -> p t d", p=128))
    while normq:
        normq.pop(0)()
    p.barrier()

    W("sync", sWst2, mW2img)
    HW4 = FT * E // 2
    p.dma("sync", sTr[1], W2fT[:, 0:HW4], w2t_hbm[:, 0:HW4])
    mW2re = p.dma("sync", sTr[1], W2fT[:, HW4:2 * HW4], w2t_hbm[:, HW4:2 * HW4])
    if mW1re is None:
        W("sync", sWst, mW1img)
        HW_ = DT * FF // 2
        p.dma("sync", sTr[0], W1fT[:, 0:HW_], w1t_hbm[:, 0:HW_])
        mW1re = p.dma("sync", sTr[0], W1fT[:, HW_:2 * HW_],
                      w1t_hbm[:, HW_:2 * HW_])
    mA4, mX14, mE4, mAddP, mMM4, mAdd4, mSt4 = {}, {}, {}, {}, {}, {}, {}
    for c in range(NCH):
        if c >= 2:
            W("sync", sT, mMM4[c - 2])
        mA4[c] = p.dma("sync", sLa[c % 2],
                       a4c[c % 2][:].rearrange("p (t s) -> p t s", t=DT),
                       xchunk_ap(a_hbm, c * SC, SC))
        if c >= 2:
            W("sync", sV, mAdd4[c - 2])
        mX14[c] = p.dma("sync", sLx[c % 2],
                        x1c4[c % 2][:].rearrange("p (t s) -> p t s", t=DT),
                        xchunk_ap(x1_hbm, c * SC, SC))
        W("tensor", sLa[c % 2], mA4[c])
        W("tensor", sGw[0], mWp)
        for dch in range(DT):
            pb = PPm[:, (dch % 2) * 512:(dch % 2) * 512 + SC]
            if dch >= 2:
                W("tensor", sS, mE4[(c, dch - 2)])
            elif c >= 1:
                W("tensor", sS, mE4[(c - 1, DT - 2 + dch)])
            for t in range(DT):
                EM("tensor", lambda e, c=c, dch=dch, t=t, pb=pb: e.matmul(
                    pb, wp_sb[:, t * E + dch * 128: t * E + dch * 128 + 128],
                    a4c[c % 2][:, t * SC:(t + 1) * SC],
                    start=(t == 0), stop=(t == DT - 1)),
                    inc=(sT if t == DT - 1 else None))
            W("scalar", sT)
            if dch >= 2:
                W("scalar", sV, mAddP[(c, dch - 2)])
            elif c >= 1:
                W("scalar", sV, mAddP[(c - 1, DT - 2 + dch)])
            mE4[(c, dch)] = EM("scalar", lambda e, dch=dch, pb=pb: e.activation(
                e4c[dch % 2][:], pb, AF.Identity, bias=bp_sb[:, dch:dch + 1]), inc=sS)
            W("vector", sS, mE4[(c, dch)])
            W("vector", sLx[c % 2], mX14[c])
            if dch == 0 and c >= 2:
                W("vector", sSt[c % 2], mSt4[c - 2])
            mAddP[(c, dch)] = EM("vector", lambda e, c=c, dch=dch: e.tensor_add(
                x2oc[c % 2][:, dch * SC:(dch + 1) * SC], e4c[dch % 2][:],
                x1c4[c % 2][:, dch * SC:(dch + 1) * SC]), inc=sV)
        mMM4[c] = p.now(sT)
        mAdd4[c] = p.now(sV)
        W("gpsimd", sV, mAdd4[c])
        mSt4[c] = p.dma("gpsimd", sSt[c % 2],
                        xchunk_ap(x2_hbm, c * SC, SC),
                        x2oc[c % 2][:].rearrange("p (t s) -> p t s", t=DT))
    p.barrier()

    W("tensor", sTr[0], mW1re)

    def _p5_w2():
        W("tensor", sTr[1], mW2re)
    ffn_phase(x2_hbm, outT, pre_mm2=_p5_w2)

    for par in range(2):
        p.wait("gpsimd", sSt[par], p.now(sSt[par]))

    p.build()
    return nc


def kernel(L0_attn_x, L0_attn_W1, L0_attn_b1, L0_attn_Wp, L0_attn_bp,
           L0_ffn_W1, L0_ffn_b1, L0_ffn_W2, L0_ffn_b2):
    x = np.asarray(L0_attn_x, dtype=np.float32)
    B, S, _ = x.shape
    nc = build_nc(S)
    w = {
        "W1a": np.ascontiguousarray(np.asarray(L0_attn_W1, np.float32)),
        "b1a": np.ascontiguousarray(np.asarray(L0_attn_b1, np.float32)),
        "Wp": np.ascontiguousarray(np.asarray(L0_attn_Wp, np.float32)),
        "bp": np.ascontiguousarray(np.asarray(L0_attn_bp, np.float32)),
        "W1f": np.ascontiguousarray(np.asarray(L0_ffn_W1, np.float32)),
        "b1f": np.ascontiguousarray(np.asarray(L0_ffn_b1, np.float32)),
        "W2f": np.ascontiguousarray(np.asarray(L0_ffn_W2, np.float32)),
        "b2f": np.ascontiguousarray(np.asarray(L0_ffn_b2, np.float32)),
    }
    in_maps = []
    for b in range(B):
        m = dict(w)
        m["xT_in"] = np.ascontiguousarray(x[b].T)
        in_maps.append(m)
    res = run_bass_kernel_spmd(nc, in_maps, core_ids=list(range(B)))
    out = np.stack([np.asarray(res.results[b]["outT"]).T for b in range(B)])
    return np.ascontiguousarray(out)
